# version 23
# baseline (speedup 1.0000x reference)
import numpy as np
import concourse.bass as bass
import concourse.mybir as mybir
from concourse import bass_utils as _bu
from concourse.bass_utils import run_bass_kernel_spmd
from contextlib import ExitStack

if not getattr(_bu, "_ldw_patched", False):
    _orig_run_command = _bu.run_command

    def _run_command_ldw(argv, **kw):
        argv = ["--enable-ldw-opt=true" if a == "--enable-ldw-opt=false" else a
                for a in argv]
        return _orig_run_command(argv, **kw)

    _bu.run_command = _run_command_ldw
    _bu._ldw_patched = True

F32 = mybir.dt.float32
BF16 = mybir.dt.bfloat16
I32 = mybir.dt.int32
U8 = mybir.dt.uint8
AF = mybir.ActivationFunctionType
ALU = mybir.AluOpType
AX = mybir.AxisListType

DEBUG = False
T = 2048
D = 1024
NB = 16
NHC = 24
EPS = 1e-5
SC127 = 1.0 / 127.0
SC127SQ = 1.0 / 16129.0


class Arena:
    def __init__(self, nc, name, total):
        self.nc = nc
        self.handle = nc.alloc_sbuf_tensor(name, [128, total], U8)
        self.base = nc.lookup_mloc(self.handle).addr
        self.total = total

    def at(self, name, shape, dtype, off):
        per_part = 1
        for s in shape[1:]:
            per_part *= s
        per_part *= mybir.dt.size(dtype)
        assert off % 32 == 0, (name, off)
        assert off + per_part <= self.total, (name, off, per_part, self.total)
        h = self.nc.virtual_sbuf_tensor(name, shape, dtype)
        self.nc._tensor(h.name, shape, dtype, type="SB")
        mloc = self.nc.lookup_mloc(h)
        mloc.allocated = True
        mloc.addr = self.base + off
        return h


class S:
    def __init__(self, h):
        self.h = h
        self.v = 0

    def inc(self, instr, n=1):
        instr.then_inc(self.h, n)
        self.v += n
        return self.v


def build_nc(prev=None):
    marks = {}

    def rec(key, val):
        marks[key] = val
        return val

    P = prev or {}

    def pm(*key):
        return P.get(key, 0)

    nc = bass.Bass("TRN2", target_bir_lowering=False)

    x_d = nc.dram_tensor("x", [T, D], F32, kind="ExternalInput").ap()
    w_in_d = nc.dram_tensor("w_in", [D, 3 * D], F32, kind="ExternalInput").ap()
    b_in_d = nc.dram_tensor("b_in", [3 * D], F32, kind="ExternalInput").ap()
    wconv_d = nc.dram_tensor("w_conv", [3 * D, 3], F32, kind="ExternalInput").ap()
    bconv_d = nc.dram_tensor("b_conv", [3 * D], F32, kind="ExternalInput").ap()
    w1_d = nc.dram_tensor("w1", [3, 64], F32, kind="ExternalInput").ap()
    b1_d = nc.dram_tensor("b1", [64], F32, kind="ExternalInput").ap()
    w2_d = nc.dram_tensor("w2", [64, 64], F32, kind="ExternalInput").ap()
    b2_d = nc.dram_tensor("b2", [64], F32, kind="ExternalInput").ap()
    w3_d = nc.dram_tensor("w3", [64, D], F32, kind="ExternalInput").ap()
    fb_d = nc.dram_tensor("filt_bias", [D], F32, kind="ExternalInput").ap()
    wout_d = nc.dram_tensor("w_out", [D, D], F32, kind="ExternalInput").ap()
    bout_d = nc.dram_tensor("b_out", [D], F32, kind="ExternalInput").ap()
    wgT_d = nc.dram_tensor("wgT", [D, 3 * D], I32, kind="ExternalInput").ap()
    wuT_d = nc.dram_tensor("wuT", [D, 3 * D], I32, kind="ExternalInput").ap()
    wdT_d = nc.dram_tensor("wdT", [3 * D, D], I32, kind="ExternalInput").ap()
    out_d = nc.dram_tensor("out", [T, D], F32, kind="ExternalOutput").ap()
    hpad_d = nc.dram_tensor("hpad", [1280], F32, kind="Internal").ap()
    if DEBUG:
        dbg = {
            "dbg_hf": nc.dram_tensor("dbg_hf", [128, 8], F32, kind="ExternalOutput").ap(),
            "dbg_H": nc.dram_tensor("dbg_H", [128, 9, 128], BF16, kind="ExternalOutput").ap(),
            "dbg_fbrep": nc.dram_tensor("dbg_fbrep", [128, 1024], F32, kind="ExternalOutput").ap(),
            "dbg_xnT": nc.dram_tensor("dbg_xnT", [128, 8, T], BF16, kind="ExternalOutput").ap(),
            "dbg_s0": nc.dram_tensor("dbg_s0", [128, 8, T], BF16, kind="ExternalOutput").ap(),
            "dbg_vinT": nc.dram_tensor("dbg_vinT", [128, 16, 1024], BF16, kind="ExternalOutput").ap(),
            "dbg_v2": nc.dram_tensor("dbg_v2", [128, 8, T], BF16, kind="ExternalOutput").ap(),
            "dbg_h0": nc.dram_tensor("dbg_h0", [128, 24, 1024], BF16, kind="ExternalOutput").ap(),
            "dbg_u0": nc.dram_tensor("dbg_u0", [128, 2050], F32, kind="ExternalOutput").ap(),
            "dbg_s0c0": nc.dram_tensor("dbg_s0c0", [128, 2048], F32, kind="ExternalOutput").ap(),
            "dbg_stats": nc.dram_tensor("dbg_stats", [128, 6, 16], F32, kind="ExternalOutput").ap(),
        }

    ones1 = nc.alloc_sbuf_tensor("ones1", [1, 128], F32).ap()
    z128 = nc.alloc_sbuf_tensor("z128", [1, 128], F32).ap()
    fbrow = nc.alloc_sbuf_tensor("fbrow", [1, 1024], F32).ap()
    boutrow = nc.alloc_sbuf_tensor("boutrow", [1, 1024], F32).ap()
    b1c = nc.alloc_sbuf_tensor("b1c", [64, 1], F32).ap()
    b2c = nc.alloc_sbuf_tensor("b2c", [64, 1], F32).ap()
    h1c = nc.alloc_sbuf_tensor("h1c", [64, 1], F32).ap()
    w1s = nc.alloc_sbuf_tensor("w1s", [3, 64], F32).ap()
    z3 = nc.alloc_sbuf_tensor("z3", [3, 1], F32).ap()
    w2t = nc.alloc_sbuf_tensor("w2t", [64, 64], F32).ap()
    w3t = nc.alloc_sbuf_tensor("w3t", [64, 1024], F32).ap()
    eps128 = nc.alloc_sbuf_tensor("eps128", [128, 1], F32).ap()
    nc.const_aps.register(F32, EPS, eps128)

    ar = Arena(nc, "arena", 198208)

    def alloc(name, shape, dtype, at):
        return ar.at(name, shape, dtype, at)

    identb = alloc("identb", [128, 128], BF16, 0).ap()
    identf = alloc("identf", [128, 128], F32, 256).ap()
    Jb = alloc("Jb", [128, 128], BF16, 768).ap()
    Hs = alloc("Hs", [128, 9, 128], BF16, 1024).ap()
    bincols = alloc("bincols", [128, 16], F32, 3328).ap()
    bccols = alloc("bccols", [128, 16], F32, 3392).ap()
    wtap = alloc("wtap", [128, 16, 3], F32, 3456).ap()
    CONST_END = 3648

    xnT = alloc("xnT", [128, 8, T], BF16, CONST_END).ap()
    R0 = CONST_END + 32768

    K32 = 32768
    v2_s = alloc("v2", [128, 8, T], BF16, R0 + 0).ap()
    s0_s = alloc("s0", [128, 8, T], BF16, R0 + K32).ap()
    vinT = alloc("vinT", [128, 16, 1024], BF16, R0 + 2 * K32).ap()
    h_buf = alloc("h_buf", [128, 24, 1024], BF16, R0 + K32).ap()
    woutb = alloc("woutb", [128, 8, 1024], BF16, R0 + K32 + 49152).ap()
    wdb = alloc("wdb", [128, 24, 1024], BF16, R0 + 3 * K32).ap()
    fbrep = alloc("fbrep", [128, 1024], F32, R0 + 131072).ap()
    ES = R0 + 147456
    wg_b0 = alloc("wg_b0", [128, 8, 128], BF16, ES).ap()
    wu_b0 = alloc("wu_b0", [128, 8, 128], BF16, ES + 2048).ap()
    sil_s = alloc("sil", [128, 1024], F32, ES + 4096).ap()
    wg_b1 = alloc("wg_b1", [128, 8, 128], BF16, ES + 8192).ap()
    wu_b1 = alloc("wu_b1", [128, 8, 128], BF16, ES + 10240).ap()
    wgbs = [wg_b0, wg_b1]
    wubs = [wu_b0, wu_b1]
    wg_p0 = alloc("wg_p0", [128, 8, 128], BF16, 768).ap()
    wu_p0 = alloc("wu_p0", [128, 8, 128], BF16, 196160).ap()
    xld2 = alloc("xld2", [128, 1024], F32, ES + 4096).ap()
    outst = alloc("outst", [128, 1024], F32, ES + 8192).ap()
    xld = alloc("xld", [128, 4, 1024], F32, R0 + 0).ap()
    xnch = alloc("xnch", [128, 2, 1024], BF16, R0 + 16384).ap()
    bns = alloc("bns", [128, 2, 6], F32, R0 + 20480).ap()
    mv = alloc("mv", [128, 16, 2], F32, R0 + 20544).ap()
    stats = alloc("stats", [128, 6, 16], F32, R0 + 20672).ap()
    hfsb = alloc("hfsb", [128, 8], F32, R0 + 21056).ap()
    hks = alloc("hks", [128, 9, 128], BF16, R0 + 21088).ap()
    wbuf = alloc("wbuf", [128, 2, 8, 128], BF16, R0 + 0).ap()
    urow0 = alloc("urow0", [128, 2050], F32, R0 + 4096).ap()
    urow1 = alloc("urow1", [128, 2050], F32, R0 + 12320).ap()
    s_a0 = alloc("s_a0", [128, 2048], F32, R0 + 20544).ap()
    s_a1 = alloc("s_a1", [128, 2048], F32, R0 + 135168).ap()
    s_as = [s_a0, s_a1]
    C2 = R0 + 3 * K32
    s_b = alloc("s_b", [128, 2048], F32, C2).ap()
    s2t = alloc("s2t", [128, 2048], F32, C2 + 8192).ap()
    vinbf = alloc("vinbf", [128, 2, 2048], BF16, C2 + 16384).ap()
    skpt = alloc("skpt", [128, 1024], F32, C2).ap()
    vscr = alloc("vscr", [128, 2, 1024], F32, C2 + 4096).ap()

    urows = [urow0, urow1]

    streams = {"g": [], "y": [], "t": [], "s": [], "v": []}

    def em(which, fn):
        streams[which].append(fn)

    with ExitStack() as ctx:
        E = ctx.enter_context
        P01 = E(nc.psum_tensor("P01", [128, 1024], F32)).ap()
        P23 = E(nc.psum_tensor("P23", [128, 1024], F32)).ap()
        P45 = E(nc.psum_tensor("P45", [128, 1024], F32)).ap()
        P67 = E(nc.psum_tensor("P67", [128, 1024], F32)).ap()
        qG = S(E(nc.semaphore("qG")))
        qS = S(E(nc.semaphore("qS")))
        qT = S(E(nc.semaphore("qT")))
        qXA = [S(E(nc.semaphore(f"qXA{i}"))) for i in range(4)]
        qGU = [S(E(nc.semaphore(f"qGU{i}"))) for i in range(2)]
        qP = S(E(nc.semaphore("qP")))
        qWB = [qG, qG]
        qD = qG
        qH = qG
        qX2 = qXA[0]
        pe = S(E(nc.semaphore("pe")))
        dv = S(E(nc.semaphore("dv")))
        ac = S(E(nc.semaphore("ac")))
        gp = S(E(nc.semaphore("gp")))
        blk = E(nc.Block())

        def bf16v(ps, lo, n):
            return ps[:, lo:lo + n].bitcast(BF16)

        def drain(eng, sem, instr):
            sem.inc(instr)
            eng.wait_ge(sem.h, sem.v)

        def fmark(eng, sem, instr):
            sem.inc(instr)
            eng.wait_ge(sem.h, sem.v)
            return sem.inc(eng.nop())

        def W(eng, sem, val):
            if val and val > 0:
                eng.wait_ge(sem.h, val)

        def g_all(g):
            for j in (1, 3):
                qXA[j % 4].inc(g.dma_start(out=xld[:, j % 4, :],
                                           in_=x_d[128 * j:128 * j + 128, :]), 16)
                rec(("qsx", j), qXA[j % 4].v)
            g.memset(ones1, 1.0)
            g.memset(z128, 0.0)
            drain(g, gp, g.memset(identb, 0.0))
            g.affine_select(out=identb, in_=identb, compare_op=ALU.not_equal,
                            fill=1.0, base=0, pattern=[[-1, 128]], channel_multiplier=1)
            drain(g, gp, g.memset(identf, 0.0))
            g.affine_select(out=identf, in_=identf, compare_op=ALU.not_equal,
                            fill=1.0, base=0, pattern=[[-1, 128]], channel_multiplier=1)
            drain(g, gp, g.memset(Jb, 0.0))
            g.memset(z3, 0.0)
            g.memset(eps128, EPS)
            i = g.affine_select(out=Jb, in_=Jb, compare_op=ALU.not_equal,
                                fill=1.0, base=-127, pattern=[[1, 128]], channel_multiplier=1)
            rec(("gp_builds",), gp.inc(i))
            g.dma_start(out=w1s, in_=w1_d).then_inc(qG.h, 16)
            g.dma_start(out=b1c, in_=b1_d.rearrange("(p f) -> p f", f=1)).then_inc(qG.h, 16)
            g.dma_start(out=b2c, in_=b2_d.rearrange("(p f) -> p f", f=1)).then_inc(qG.h, 16)
            g.dma_start(out=w2t, in_=w2_d).then_inc(qG.h, 16)
            g.dma_start(out=w3t, in_=w3_d).then_inc(qG.h, 16)
            g.dma_start(out=fbrow, in_=fb_d.rearrange("(p f) -> p f", p=1)).then_inc(qG.h, 16)
            g.dma_start(out=boutrow, in_=bout_d.rearrange("(p f) -> p f", p=1)).then_inc(qG.h, 16)
            qG.v += 7 * 16
            rec(("qg_hfilt",), qG.v)
            g.wait_ge(qG.h, qG.v)

            for lo, hi in ((0, 8), (8, 16)):
                eo = 0 if lo == 0 else 2048
                g.dma_start(out=bincols[:, lo:hi],
                            in_=b_in_d[eo:eo + 1024].rearrange("(a p) -> p a", p=128),
                            allow_slow_non_contiguous=True).then_inc(qG.h, 16)
                g.dma_start(out=bccols[:, lo:hi],
                            in_=bconv_d[eo:eo + 1024].rearrange("(a p) -> p a", p=128),
                            allow_slow_non_contiguous=True).then_inc(qG.h, 16)
                g.dma_start(out=wtap[:, lo:hi, :],
                            in_=wconv_d.rearrange("(a p) t -> p a t", p=128)[:, eo // 128:eo // 128 + 8, :]).then_inc(qG.h, 16)
                qG.v += 3 * 16
            g.wait_ge(qG.h, qG.v)

            for j in range(5, NB, 2):
                W(g, ac, pm("xnc", j - 4))
                qXA[j % 4].inc(g.dma_start(out=xld[:, j % 4, :],
                                           in_=x_d[128 * j:128 * j + 128, :]), 16)
                rec(("qsx", j), qXA[j % 4].v)

            W(g, ac, pm("ac_hfev"))
            g.dma_start(out=hpad_d[128:1152].rearrange("(a p) -> p a", p=128),
                        in_=hfsb, allow_slow_non_contiguous=True).then_inc(qG.h, 16)
            g.dma_start(out=hpad_d[0:128].rearrange("(p f) -> p f", p=1), in_=z128).then_inc(qG.h, 16)
            g.dma_start(out=hpad_d[1152:1280].rearrange("(p f) -> p f", p=1), in_=z128).then_inc(qG.h, 16)
            qG.v += 3 * 16
            g.wait_ge(qG.h, qG.v)
            rec(("qg_a0",), qG.v)
            hank = bass.AP(tensor=hpad_d.tensor, offset=1, ap=[[1, 128], [128, 9], [1, 128]])
            g.dma_start(out=hks, in_=hank).then_inc(qH.h, 16)
            qH.v += 16
            rec(("qg_hk",), qH.v)
            g.wait_ge(qH.h, qH.v)

            def g_vin(cc2):
                W(g, dv, pm("taps", 2 * cc2 + 1))
                if cc2 >= 2:
                    W(g, pe, pm("vt", cc2 - 2))
                i = g.tensor_tensor(out=vinbf[:, cc2 % 2, :], in0=s2t,
                                    in1=s0_s[:, cc2, :], op=ALU.mult)
                rec(("vin", cc2), fmark(g, gp, i))

            for k in range(16):
                cs, sh = k // 2, k % 2
                eoff = 128 * cs if sh == 0 else 2048 + 128 * cs
                if k < 2:
                    W(g, dv, pm("trev", 15))
                    W(g, dv, pm("st", 15))
                else:
                    W(g, pe, pm("umm", k - 2))
                if k >= 1:
                    W(g, qG, pm("qgw", k - 1))
                qG.inc(g.dma_start(out=wbuf[:, k % 2],
                                   in_=w_in_d.rearrange("(a p) e -> p a e", p=128)[:, :, eoff:eoff + 128]), 16)
                rec(("qgw", k), qG.v)
                if k >= 4 and k % 2 == 0:
                    g_vin((k - 4) // 2)
            for cc2 in (6, 7):
                g_vin(cc2)

            for s in range(2):
                for hc in range(NHC):
                    if s == 0 and hc == 2:
                        W(g, pe, pm("y", 15))
                        qP.inc(g.dma_start(
                            out=wg_p0,
                            in_=wgT_d.rearrange("(a p) h -> p a h", p=128)[:, :, 0:128]), 16)
                        qP.inc(g.dma_start(
                            out=wu_p0,
                            in_=wuT_d.rearrange("(a p) h -> p a h", p=128)[:, :, 0:128]), 16)
                    if s == 1 and hc == 0:
                        continue
                    slot = hc % 2 if s == 0 else (hc + 1) % 2
                    if s == 1 and hc == 1:
                        W(g, pe, pm("gu", 0, 22))
                    if s == 1 and hc == 2:
                        W(g, pe, pm("gu", 0, 23))
                        W(g, qT, pm("qtst", 0, 7))
                    if (s == 0 and hc >= 2) or (s == 1 and hc >= 3):
                        W(g, pe, pm("gu", s, hc - 2))
                    qGU[slot].inc(g.dma_start(
                        out=wgbs[slot],
                        in_=wgT_d.rearrange("(a p) h -> p a h", p=128)[:, :, 128 * hc:128 * hc + 128]), 16)
                    qGU[slot].inc(g.dma_start(
                        out=wubs[slot],
                        in_=wuT_d.rearrange("(a p) h -> p a h", p=128)[:, :, 128 * hc:128 * hc + 128]), 16)
                    rec(("qggu", s, hc), qGU[slot].v)

                    if s == 0 and 4 <= hc <= 19:
                        if hc == 4:
                            W(g, pe, pm("vtr", 15))
                            W(g, dv, pm("gate", 15))
                        if hc < 16:
                            c2 = 2 * (hc - 4)
                            qD.inc(g.dma_start(
                                out=wdb[:, c2:c2 + 2, :],
                                in_=wdT_d.rearrange("(a p) d -> p a d", p=128)[:, c2:c2 + 2, :]), 16)
                        else:
                            c2 = 2 * (hc - 16)
                            qD.inc(g.dma_start(
                                out=woutb[:, c2:c2 + 2, :],
                                in_=wout_d.rearrange("(a p) d -> p a d", p=128)[:, c2:c2 + 2, :]), 16)
                        rec(("qgwd",), qD.v)
        em("g", g_all)

        def y_all(sy):
            for j in range(0, NB, 2):
                if j >= 4:
                    W(sy, ac, pm("xnc", j - 4))
                qXA[j % 4].inc(sy.dma_start(out=xld[:, j % 4, :],
                                            in_=x_d[128 * j:128 * j + 128, :]), 16)
                rec(("qsx", j), qXA[j % 4].v)

            if DEBUG:
                W(sy, ac, pm("a0ev"))
                qT.inc(sy.dma_start(out=dbg["dbg_hf"], in_=hfsb), 16)
                qT.inc(sy.dma_start(out=dbg["dbg_H"], in_=Hs), 16)
                qT.inc(sy.dma_start(out=dbg["dbg_fbrep"], in_=fbrep), 16)
                W(sy, dv, pm("trev", 15))
                qT.inc(sy.dma_start(out=dbg["dbg_xnT"], in_=xnT), 16)
                qT.inc(sy.dma_start(out=dbg["dbg_stats"], in_=stats), 16)
                W(sy, ac, pm("uev", 0))
                qT.inc(sy.dma_start(out=dbg["dbg_u0"], in_=urow0), 16)
                W(sy, dv, pm("taps", 14))
                qT.inc(sy.dma_start(out=dbg["dbg_s0"], in_=s0_s), 16)
                W(sy, ac, pm("vtev", 7))
                qT.inc(sy.dma_start(out=dbg["dbg_vinT"], in_=vinT), 16)
                W(sy, dv, pm("gate", 15))
                qT.inc(sy.dma_start(out=dbg["dbg_v2"], in_=v2_s), 16)
                W(sy, dv, pm("h", 0, 23))
                qT.inc(sy.dma_start(out=dbg["dbg_h0"], in_=h_buf), 16)

            for s in range(2):
                for jl in range(8):
                    j = 8 * s + jl
                    if jl == 0:
                        W(sy, dv, pm("h", s, 23))
                    if jl >= 1 or s == 1:
                        W(sy, dv, pm("oev", s, jl - 1) if jl >= 1 else pm("oev", 0, 7))
                    qX2.inc(sy.dma_start(out=xld2, in_=x_d[128 * j:128 * j + 128, :]), 16)
                    rec(("qsx2", s, jl), qX2.v)
                    W(sy, dv, pm("oev", s, jl))
                    qT.inc(sy.dma_start(out=out_d[128 * j:128 * j + 128, :], in_=outst), 16)
                    rec(("qtst", s, jl), qT.v)
            sy.wait_ge(qT.h, qT.v)
        em("y", y_all)

        def t_all(t):
            W(t, qG, pm("qg_hfilt"))
            W(t, gp, pm("gp_builds"))
            rec(("pe_w1",), pe.inc(t.matmul(P01[0:64, 1:2], w1s, z3, start=True, stop=True)))
            W(t, ac, pm("ac_h1"))
            rec(("pe_ps1",), pe.inc(t.matmul(P01[0:64, 0:1], w2t, h1c, start=True, stop=True)))
            W(t, ac, pm("ac_h2"))
            for j in range(8):
                i = t.matmul(P23[:, j:j + 1], w3t[:, 128 * j:128 * j + 128], b1c,
                             start=True, stop=True)
            rec(("pe_hf",), pe.inc(i))

            for j in range(NB):
                W(t, ac, pm("xnc", j))
                if j == 0:
                    W(t, ac, pm("ac_h2"))
                elif j == 1:
                    W(t, ac, pm("ac_hfev"))
                else:
                    W(t, ac if j % 2 == 0 else dv, pm("trev", j - 2))
                ps = P01 if j % 2 == 0 else P23
                view = bf16v(ps, 0, 512).rearrange("p (a f) -> p a f", f=128)
                for a in range(8):
                    i = t.transpose(view[:, a, :], xnch[:, j % 2, 128 * a:128 * a + 128], identb)
                rec(("tr", j), pe.inc(i))

            W(t, qH, pm("qg_hk"))
            W(t, dv, pm("trev", 15))
            for d in range(8):
                t.matmul(P01[:, 128 * d:128 * d + 128], Jb, hks[:, d, :], start=True, stop=True)
            t.matmul(P23[:, 0:128], Jb, hks[:, 8, :], start=True, stop=True)
            t.matmul(P45[:, 0:512], ones1, fbrow[:, 0:512], start=True, stop=True)
            i = t.matmul(P45[:, 512:1024], ones1, fbrow[:, 512:1024], start=True, stop=True)
            rec(("pe_a0",), pe.inc(i))

            for it in range(18):
                if it < 16:
                    k = it
                    cs, sh = k // 2, k % 2
                    W(t, qWB[k % 2], pm("qgw", k))
                    if k < 2:
                        W(t, ac, pm("a0ev"))
                    else:
                        W(t, ac, pm("uev", k - 2))
                    if sh == 1 and cs >= 2:
                        W(t, ac, pm("vtev", cs - 2))
                    pair = (P01, P23) if k % 2 == 0 else (P45, P67)
                    tiles = [pair[0][:, 0:512], pair[0][:, 512:1024],
                             pair[1][:, 0:512], pair[1][:, 512:1024]]
                    for a in range(8):
                        for tc in range(4):
                            i = t.matmul(tiles[tc], wbuf[:, k % 2, a, :],
                                         xnT[:, a, 512 * tc:512 * tc + 512],
                                         start=(a == 0), stop=(a == 7))
                    rec(("umm", k), pe.inc(i))
                if (it >= 4 and it % 2 == 0) or it == 17:
                    cc = (it - 4) // 2 if it < 17 else 7
                    W(t, gp, pm("vin", cc))
                    if cc >= 1:
                        W(t, ac, pm("vtev", cc - 1))
                    W(t, ac, pm("uev", min(2 * cc + 3, 15)))
                    view = bf16v(P45, 0, 1024).rearrange("p (a f) -> p a f", f=128)
                    for jt in range(16):
                        i = t.transpose(view[:, jt, :],
                                        vinbf[:, cc % 2, 128 * jt:128 * jt + 128], identb)
                    rec(("vt", cc), pe.inc(i))

            for j in range(NB + 1):
                if j < NB:
                    if j < 2:
                        W(t, ac, pm("uev", 14 + j))
                        W(t, ac, pm("vtev", 7))
                    else:
                        W(t, dv, pm("vs", j - 2))
                    psy = P01 if j % 2 == 0 else P23
                    dmax = min(j, 8)
                    for n2 in range(2):
                        for d in range(dmax + 1):
                            i = t.matmul(psy[:, 512 * n2:512 * n2 + 512], Hs[:, d, :],
                                         vinT[:, j - d, 512 * n2:512 * n2 + 512],
                                         start=(d == 0), stop=(d == dmax))
                    rec(("y", j), pe.inc(i))
                if j >= 1:
                    jj = j - 1
                    W(t, dv, pm("vs", jj))
                    if jj >= 2:
                        W(t, dv, pm("gate", jj - 2))
                    else:
                        W(t, ac, pm("uev", 15))
                        W(t, ac, pm("vtev", 7))
                    pt = P45 if jj % 2 == 0 else P67
                    ptv = pt.rearrange("p (a f) -> p a f", f=128)
                    for q in range(8):
                        i = t.transpose(ptv[:, q, :], vscr[:, jj % 2, 128 * q:128 * q + 128], identf)
                    rec(("vtr", jj), pe.inc(i))

            for s in range(2):
                scol = 1024 * s
                for hc in range(NHC):
                    if s == 1 and hc == 0:
                        W(t, qP, 32)
                        wg_src, wu_src = wg_p0, wu_p0
                    else:
                        slot = hc % 2 if s == 0 else (hc + 1) % 2
                        W(t, qGU[slot], pm("qggu", s, hc))
                        wg_src, wu_src = wgbs[slot], wubs[slot]
                    if hc >= 2:
                        W(t, ac, pm("sil", s, hc - 2))
                        W(t, dv, pm("h", s, hc - 2))
                    elif s == 0:
                        W(t, dv, pm("gate", 15))
                    else:
                        W(t, dv, pm("oev", 0, 7))
                        W(t, ac, pm("sil", 0, 22 + hc))
                        W(t, dv, pm("h", 0, 22 + hc))
                    psg = P01 if hc % 2 == 0 else P45
                    psu = P23 if hc % 2 == 0 else P67
                    for a in range(8):
                        for n2 in range(2):
                            t.matmul(psg[:, 512 * n2:512 * n2 + 512], wg_src[:, a, :],
                                     xnT[:, a, scol + 512 * n2:scol + 512 * n2 + 512],
                                     start=(a == 0), stop=(a == 7))
                    for a in range(8):
                        for n2 in range(2):
                            i = t.matmul(psu[:, 512 * n2:512 * n2 + 512], wu_src[:, a, :],
                                         xnT[:, a, scol + 512 * n2:scol + 512 * n2 + 512],
                                         start=(a == 0), stop=(a == 7))
                    rec(("gu", s, hc), pe.inc(i))

                for jl in range(8):
                    j = 8 * s + jl
                    W(t, qD, pm("qgwd"))
                    if jl >= 2:
                        W(t, dv, pm("oev", s, jl - 2))
                    else:
                        W(t, ac, pm("sil", s, 23))
                        W(t, dv, pm("h", s, 23))
                    pso = P01 if jl % 2 == 0 else P23
                    first = [True, True]
                    for cc in range(8):
                        for n2 in range(2):
                            t.matmul(pso[:, 512 * n2:512 * n2 + 512],
                                     v2_s[:, cc, 128 * j:128 * j + 128],
                                     woutb[:, cc, 512 * n2:512 * n2 + 512],
                                     start=first[n2], stop=False)
                            first[n2] = False
                    for n2 in range(2):
                        t.matmul(pso[:, 512 * n2:512 * n2 + 512], ones1,
                                 boutrow[:, 512 * n2:512 * n2 + 512], start=False, stop=False)
                    for hc in range(NHC):
                        for n2 in range(2):
                            i = t.matmul(pso[:, 512 * n2:512 * n2 + 512],
                                         h_buf[:, hc, 128 * jl:128 * jl + 128],
                                         wdb[:, hc, 512 * n2:512 * n2 + 512],
                                         start=False, stop=(hc == NHC - 1))
                    rec(("guo", s, jl), pe.inc(i))
        em("t", t_all)

        def s_all(s):
            W(s, qG, pm("qg_hfilt"))
            W(s, pe, pm("pe_w1"))
            rec(("ac_h1",), fmark(s, ac, s.activation(out=h1c, in_=P01[0:64, 1:2], func=AF.Silu,
                                                       bias=b1c, scale=1.0)))
            W(s, pe, pm("pe_ps1"))
            rec(("ac_h2",), fmark(s, ac, s.activation(out=b1c, in_=P01[0:64, 0:1], func=AF.Silu,
                                                       bias=b2c, scale=1.0)))
            W(s, pe, pm("pe_hf"))
            rec(("ac_hfev",), fmark(s, ac, s.copy(hfsb, P23[:, 0:8])))

            for it in range(NB + 2):
                if it < NB:
                    j = it
                    W(s, dv, pm("st", j))
                    if j == 0:
                        W(s, gp, pm("gp_builds"))
                    rec(("sd", j), fmark(s, ac, s.activation(out=stats[:, 5, j:j + 1],
                                                              in_=mv[:, j, 1:2],
                                                              func=AF.Sqrt, bias=EPS)))
                if 1 <= it <= NB:
                    j = it - 1
                    W(s, dv, pm("mrs", j))
                    if j >= 2:
                        W(s, pe, pm("tr", j - 2))
                    i = s.activation(out=xnch[:, j % 2, :], in_=xld[:, j % 4, :],
                                     func=AF.Identity, scale=stats[:, 4, j:j + 1],
                                     bias=stats[:, 3, j:j + 1])
                    rec(("xnc", j), ac.inc(i))
                if it >= 2 and (it - 2) % 2 == 0:
                    j = it - 2
                    W(s, pe, pm("tr", j))
                    i = s.copy(xnT[:, :, 128 * j:128 * j + 128],
                               bf16v(P01, 0, 512).rearrange("p (a f) -> p a f", f=128))
                    rec(("trev", j), ac.inc(i))

            W(s, pe, pm("pe_a0"))
            s.copy(Hs[:, 0:8, :], bf16v(P01, 0, 512).rearrange("p (a f) -> p a f", f=128))
            s.copy(Hs[:, 8, :], bf16v(P23, 0, 64))
            rec(("a0ev",), ac.inc(s.copy(fbrep, P45)))

            for k in range(16):
                cs, sh = k // 2, k % 2
                cidx = cs + 8 * sh
                if k == 0:
                    W(s, qG, pm("qg_a0"))
                W(s, pe, pm("umm", k))
                if k >= 2:
                    W(s, dv, pm("taps", k - 2))
                pair = (P01, P23) if k % 2 == 0 else (P45, P67)
                ur = urows[k % 2]
                s.activation(out=ur[:, 2:1026], in_=pair[0],
                             func=AF.Identity, bias=bincols[:, cidx:cidx + 1], scale=1.0)
                i = s.activation(out=ur[:, 1026:2050], in_=pair[1],
                                 func=AF.Identity, bias=bincols[:, cidx:cidx + 1], scale=1.0)
                drain(s, ac, i)
                i = s.activation(out=s_as[k % 2], in_=ur[:, 2:2050], func=AF.Identity,
                                 bias=bccols[:, cidx:cidx + 1], scale=wtap[:, cidx, 2:3])
                rec(("uev", k), fmark(s, ac, i))
                if (k >= 4 and k % 2 == 0) or k == 15:
                    for cc in ([(k - 4) // 2] if k < 15 else [6, 7]):
                        W(s, pe, pm("vt", cc))
                        s.copy(vinT[:, 0:8, 128 * cc:128 * cc + 128],
                               bf16v(P45, 0, 512).rearrange("p (a f) -> p a f", f=128))
                        i = s.copy(vinT[:, 8:16, 128 * cc:128 * cc + 128],
                                   bf16v(P45, 512, 512).rearrange("p (a f) -> p a f", f=128))
                        rec(("vtev", cc), ac.inc(i))

            for s2 in range(2):
                for hc in range(NHC):
                    W(s, pe, pm("gu", s2, hc))
                    if hc >= 1:
                        W(s, dv, pm("h", s2, hc - 1))
                    elif s2 == 1:
                        W(s, dv, pm("oev", 0, 7))
                    psg = P01 if hc % 2 == 0 else P45
                    rec(("sil", s2, hc), ac.inc(
                        s.activation(out=sil_s, in_=psg, func=AF.Silu, scale=SC127)))
        em("s", s_all)

        def v_all(v):
            for it in range(NB + 2):
                if it < NB:
                    j = it
                    W(v, qXA[j % 4], pm("qsx", j))
                    v.bn_stats(bns[:, 0, :], xld[:, j % 4, 0:512])
                    drain(v, dv, v.bn_stats(bns[:, 1, :], xld[:, j % 4, 512:1024]))
                    rec(("st", j), fmark(v, dv, v.bn_aggr(mv[:, j, :], bns.rearrange("p c s -> p (c s)"))))
                if 1 <= it <= NB:
                    j = it - 1
                    W(v, ac, pm("sd", j))
                    drain(v, dv, v.reciprocal(stats[:, 4, j:j + 1], stats[:, 5, j:j + 1]))
                    i = v.tensor_scalar(out=stats[:, 3, j:j + 1], in0=mv[:, j, 0:1],
                                        scalar1=stats[:, 4, j:j + 1], scalar2=-1.0,
                                        op0=ALU.mult, op1=ALU.mult)
                    rec(("mrs", j), fmark(v, dv, i))
                if it >= 2 and (it - 2) % 2 == 1:
                    j = it - 2
                    W(v, pe, pm("tr", j))
                    i = v.tensor_copy(xnT[:, :, 128 * j:128 * j + 128],
                                      bf16v(P23, 0, 512).rearrange("p (a f) -> p a f", f=128))
                    rec(("trev", j), dv.inc(i))

            W(v, pe, pm("tr", 15))
            v.memset(urow0[:, 0:2], 0.0)
            drain(v, dv, v.memset(urow1[:, 0:2], 0.0))
            for it in range(18):
                if 1 <= it <= 16:
                    kk = it - 1
                    csk, shk = kk // 2, kk % 2
                    cidxk = csk + 8 * shk
                    if kk == 0:
                        W(v, qG, pm("qg_a0"))
                    W(v, ac, pm("uev", kk))
                    drain(v, dv, v.scalar_tensor_tensor(out=s_b, in0=urows[kk % 2][:, 1:2049],
                                                        scalar=wtap[:, cidxk, 1:2],
                                                        in1=s_as[kk % 2], op0=ALU.mult, op1=ALU.add))
                    if shk == 1 and csk >= 1:
                        W(v, gp, pm("vin", csk - 1))
                    dest = s0_s[:, csk, :] if shk == 0 else s2t
                    i = v.scalar_tensor_tensor(out=dest, in0=urows[kk % 2][:, 0:2048],
                                               scalar=wtap[:, cidxk, 0:1],
                                               in1=s_b, op0=ALU.mult, op1=ALU.add)
                    if shk == 0:
                        rec(("taps", kk), dv.inc(i))
                        v.wait_ge(dv.h, dv.v)
                    else:
                        rec(("taps", kk), fmark(v, dv, i))

            for j in range(NB + 1):
                if j < NB:
                    W(v, pe, pm("y", j))
                    drain(v, dv, v.tensor_tensor(out=skpt, in0=vinT[:, j, :], in1=fbrep,
                                                 op=ALU.mult))
                    if j >= 2:
                        W(v, pe, pm("vtr", j - 2))
                    psy = P01 if j % 2 == 0 else P23
                    i = v.tensor_tensor(out=vscr[:, j % 2, :], in0=skpt, in1=psy, op=ALU.add)
                    rec(("vs", j), fmark(v, dv, i))
                if j >= 1:
                    jj = j - 1
                    W(v, pe, pm("vtr", jj))
                    pt = P45 if jj % 2 == 0 else P67
                    for h2 in range(2):
                        i = v.tensor_tensor(
                            out=v2_s[:, 4 * h2:4 * h2 + 4, 128 * jj:128 * jj + 128],
                            in0=pt[:, 512 * h2:512 * h2 + 512].rearrange("p (a f) -> p a f", f=128),
                            in1=s0_s[:, 4 * h2:4 * h2 + 4, 128 * jj:128 * jj + 128],
                            op=ALU.mult)
                    rec(("gate", jj), dv.inc(i))

            for s in range(2):
                for hc in range(NHC):
                    W(v, ac, pm("sil", s, hc))
                    psu = P23 if hc % 2 == 0 else P67
                    i = v.scalar_tensor_tensor(out=h_buf[:, hc, :], in0=psu, scalar=SC127SQ,
                                               in1=sil_s, op0=ALU.mult, op1=ALU.mult)
                    rec(("h", s, hc), dv.inc(i))
                for jl in range(8):
                    W(v, pe, pm("guo", s, jl))
                    W(v, qX2, pm("qsx2", s, jl))
                    if jl >= 1 or s == 1:
                        W(v, qT, pm("qtst", s, jl - 1) if jl >= 1 else pm("qtst", 0, 7))
                    pso = P01 if jl % 2 == 0 else P23
                    i = v.tensor_tensor(out=outst, in0=pso, in1=xld2, op=ALU.add)
                    rec(("oev", s, jl), fmark(v, dv, i))
        em("v", v_all)

        blk.gpsimd(lambda g: [f(g) for f in streams["g"]])
        blk.sync(lambda sy: [f(sy) for f in streams["y"]])
        blk.tensor(lambda t: [f(t) for f in streams["t"]])
        blk.scalar(lambda s: [f(s) for f in streams["s"]])
        blk.vector(lambda v: [f(v) for f in streams["v"]])

    return nc, marks


def kernel(**inputs):
    f32 = np.float32
    x = np.ascontiguousarray(np.asarray(inputs["L0_attn_x"], f32))
    w_in = np.ascontiguousarray(np.asarray(inputs["L0_attn_w_in"], f32))
    b_in = np.ascontiguousarray(np.asarray(inputs["L0_attn_b_in"], f32))
    w_conv = np.ascontiguousarray(np.asarray(inputs["L0_attn_w_conv"], f32).reshape(3072, 3))
    b_conv = np.ascontiguousarray(np.asarray(inputs["L0_attn_b_conv"], f32))
    w1 = np.ascontiguousarray(np.asarray(inputs["L0_attn_w1"], f32))
    b1 = np.ascontiguousarray(np.asarray(inputs["L0_attn_b1"], f32))
    w2 = np.ascontiguousarray(np.asarray(inputs["L0_attn_w2"], f32))
    b2 = np.ascontiguousarray(np.asarray(inputs["L0_attn_b2"], f32))
    w3 = np.ascontiguousarray(np.asarray(inputs["L0_attn_w3"], f32))
    fb = np.ascontiguousarray(np.asarray(inputs["L0_attn_filt_bias"], f32))
    w_out = np.ascontiguousarray(np.asarray(inputs["L0_attn_w_out"], f32))
    b_out = np.ascontiguousarray(np.asarray(inputs["L0_attn_b_out"], f32))
    wgT = np.ascontiguousarray(np.asarray(inputs["L0_ffn_w_gate"], np.int32).T)
    wuT = np.ascontiguousarray(np.asarray(inputs["L0_ffn_w_up"], np.int32).T)
    wdT = np.ascontiguousarray(np.asarray(inputs["L0_ffn_w_down"], np.int32).T)

    _, m1 = build_nc(None)
    nc, _ = build_nc(m1)
    in_maps = []
    for b in range(8):
        in_maps.append(dict(x=np.ascontiguousarray(x[b]), w_in=w_in, b_in=b_in,
                            w_conv=w_conv, b_conv=b_conv, w1=w1, b1=b1, w2=w2, b2=b2, w3=w3,
                            filt_bias=fb, w_out=w_out, b_out=b_out,
                            wgT=wgT, wuT=wuT, wdT=wdT))
    res = run_bass_kernel_spmd(nc, in_maps, core_ids=list(range(8)))
    return np.stack([np.asarray(r["out"], np.float32) for r in res.results], axis=0)


if __name__ == "__main__":
    import reference as R
    import jax
    with jax.default_device(jax.devices("cpu")[0]):
        inp = {k: np.asarray(v) for k, v in R.setup_inputs(0).items()}
    out = kernel(**inp)
    print("out", out.shape, out.dtype)


# version 24
# speedup vs baseline: 1.0122x; 1.0122x over previous
import numpy as np
import concourse.bass as bass
import concourse.mybir as mybir
from concourse import bass_utils as _bu
from concourse.bass_utils import run_bass_kernel_spmd
from contextlib import ExitStack

if not getattr(_bu, "_ldw_patched", False):
    _orig_run_command = _bu.run_command

    def _run_command_ldw(argv, **kw):
        argv = ["--enable-ldw-opt=true" if a == "--enable-ldw-opt=false" else a
                for a in argv]
        return _orig_run_command(argv, **kw)

    _bu.run_command = _run_command_ldw
    _bu._ldw_patched = True

F32 = mybir.dt.float32
BF16 = mybir.dt.bfloat16
I32 = mybir.dt.int32
U8 = mybir.dt.uint8
AF = mybir.ActivationFunctionType
ALU = mybir.AluOpType
AX = mybir.AxisListType

DEBUG = False
T = 2048
D = 1024
NB = 16
NHC = 24
EPS = 1e-5
SC127 = 1.0 / 127.0
SC127SQ = 1.0 / 16129.0


class Arena:
    def __init__(self, nc, name, total):
        self.nc = nc
        self.handle = nc.alloc_sbuf_tensor(name, [128, total], U8)
        self.base = nc.lookup_mloc(self.handle).addr
        self.total = total

    def at(self, name, shape, dtype, off):
        per_part = 1
        for s in shape[1:]:
            per_part *= s
        per_part *= mybir.dt.size(dtype)
        assert off % 32 == 0, (name, off)
        assert off + per_part <= self.total, (name, off, per_part, self.total)
        h = self.nc.virtual_sbuf_tensor(name, shape, dtype)
        self.nc._tensor(h.name, shape, dtype, type="SB")
        mloc = self.nc.lookup_mloc(h)
        mloc.allocated = True
        mloc.addr = self.base + off
        return h


class S:
    def __init__(self, h):
        self.h = h
        self.v = 0

    def inc(self, instr, n=1):
        instr.then_inc(self.h, n)
        self.v += n
        return self.v


def build_nc(prev=None):
    marks = {}

    def rec(key, val):
        marks[key] = val
        return val

    P = prev or {}

    def pm(*key):
        return P.get(key, 0)

    nc = bass.Bass("TRN2", target_bir_lowering=False)

    x_d = nc.dram_tensor("x", [T, D], F32, kind="ExternalInput").ap()
    w_in_d = nc.dram_tensor("w_in", [D, 3 * D], F32, kind="ExternalInput").ap()
    b_in_d = nc.dram_tensor("b_in", [3 * D], F32, kind="ExternalInput").ap()
    wconv_d = nc.dram_tensor("w_conv", [3 * D, 3], F32, kind="ExternalInput").ap()
    bconv_d = nc.dram_tensor("b_conv", [3 * D], F32, kind="ExternalInput").ap()
    w1_d = nc.dram_tensor("w1", [3, 64], F32, kind="ExternalInput").ap()
    b1_d = nc.dram_tensor("b1", [64], F32, kind="ExternalInput").ap()
    w2_d = nc.dram_tensor("w2", [64, 64], F32, kind="ExternalInput").ap()
    b2_d = nc.dram_tensor("b2", [64], F32, kind="ExternalInput").ap()
    w3_d = nc.dram_tensor("w3", [64, D], F32, kind="ExternalInput").ap()
    fb_d = nc.dram_tensor("filt_bias", [D], F32, kind="ExternalInput").ap()
    wout_d = nc.dram_tensor("w_out", [D, D], F32, kind="ExternalInput").ap()
    bout_d = nc.dram_tensor("b_out", [D], F32, kind="ExternalInput").ap()
    wgT_d = nc.dram_tensor("wgT", [D, 3 * D], I32, kind="ExternalInput").ap()
    wuT_d = nc.dram_tensor("wuT", [D, 3 * D], I32, kind="ExternalInput").ap()
    wdT_d = nc.dram_tensor("wdT", [3 * D, D], I32, kind="ExternalInput").ap()
    out_d = nc.dram_tensor("out", [T, D], F32, kind="ExternalOutput").ap()
    hpad_d = nc.dram_tensor("hpad", [1280], F32, kind="Internal").ap()
    if DEBUG:
        dbg = {
            "dbg_hf": nc.dram_tensor("dbg_hf", [128, 8], F32, kind="ExternalOutput").ap(),
            "dbg_H": nc.dram_tensor("dbg_H", [128, 9, 128], BF16, kind="ExternalOutput").ap(),
            "dbg_fbrep": nc.dram_tensor("dbg_fbrep", [128, 1024], F32, kind="ExternalOutput").ap(),
            "dbg_xnT": nc.dram_tensor("dbg_xnT", [128, 8, T], BF16, kind="ExternalOutput").ap(),
            "dbg_s0": nc.dram_tensor("dbg_s0", [128, 8, T], BF16, kind="ExternalOutput").ap(),
            "dbg_vinT": nc.dram_tensor("dbg_vinT", [128, 16, 1024], BF16, kind="ExternalOutput").ap(),
            "dbg_v2": nc.dram_tensor("dbg_v2", [128, 8, T], BF16, kind="ExternalOutput").ap(),
            "dbg_h0": nc.dram_tensor("dbg_h0", [128, 24, 1024], BF16, kind="ExternalOutput").ap(),
            "dbg_u0": nc.dram_tensor("dbg_u0", [128, 2050], F32, kind="ExternalOutput").ap(),
            "dbg_s0c0": nc.dram_tensor("dbg_s0c0", [128, 2048], F32, kind="ExternalOutput").ap(),
            "dbg_stats": nc.dram_tensor("dbg_stats", [128, 6, 16], F32, kind="ExternalOutput").ap(),
        }

    ones1 = nc.alloc_sbuf_tensor("ones1", [1, 128], F32).ap()
    z128 = nc.alloc_sbuf_tensor("z128", [1, 128], F32).ap()
    fbrow = nc.alloc_sbuf_tensor("fbrow", [1, 1024], F32).ap()
    boutrow = nc.alloc_sbuf_tensor("boutrow", [1, 1024], F32).ap()
    b1c = nc.alloc_sbuf_tensor("b1c", [64, 1], F32).ap()
    b2c = nc.alloc_sbuf_tensor("b2c", [64, 1], F32).ap()
    h1c = nc.alloc_sbuf_tensor("h1c", [64, 1], F32).ap()
    w1s = nc.alloc_sbuf_tensor("w1s", [3, 64], F32).ap()
    z3 = nc.alloc_sbuf_tensor("z3", [3, 1], F32).ap()
    w2t = nc.alloc_sbuf_tensor("w2t", [64, 64], F32).ap()
    w3t = nc.alloc_sbuf_tensor("w3t", [64, 1024], F32).ap()
    eps128 = nc.alloc_sbuf_tensor("eps128", [128, 1], F32).ap()
    nc.const_aps.register(F32, EPS, eps128)

    ar = Arena(nc, "arena", 198208)

    def alloc(name, shape, dtype, at):
        return ar.at(name, shape, dtype, at)

    identb = alloc("identb", [128, 128], BF16, 0).ap()
    identf = alloc("identf", [128, 128], F32, 256).ap()
    Jb = alloc("Jb", [128, 128], BF16, 768).ap()
    Hs = alloc("Hs", [128, 9, 128], BF16, 1024).ap()
    bincols = alloc("bincols", [128, 16], F32, 3328).ap()
    bccols = alloc("bccols", [128, 16], F32, 3392).ap()
    wtap = alloc("wtap", [128, 16, 3], F32, 3456).ap()
    CONST_END = 3648

    xnT = alloc("xnT", [128, 8, T], BF16, CONST_END).ap()
    R0 = CONST_END + 32768

    K32 = 32768
    v2_s = alloc("v2", [128, 8, T], BF16, R0 + 0).ap()
    s0_s = alloc("s0", [128, 8, T], BF16, R0 + K32).ap()
    vinT = alloc("vinT", [128, 16, 1024], BF16, R0 + 2 * K32).ap()
    h_buf = alloc("h_buf", [128, 24, 1024], BF16, R0 + K32).ap()
    woutb = alloc("woutb", [128, 8, 1024], BF16, R0 + K32 + 49152).ap()
    wdb = alloc("wdb", [128, 24, 1024], BF16, R0 + 3 * K32).ap()
    fbrep = alloc("fbrep", [128, 1024], F32, R0 + 131072).ap()
    ES = R0 + 147456
    wg_b0 = alloc("wg_b0", [128, 8, 128], BF16, ES).ap()
    wu_b0 = alloc("wu_b0", [128, 8, 128], BF16, ES + 2048).ap()
    sil_s = alloc("sil", [128, 1024], F32, ES + 4096).ap()
    wg_b1 = alloc("wg_b1", [128, 8, 128], BF16, ES + 8192).ap()
    wu_b1 = alloc("wu_b1", [128, 8, 128], BF16, ES + 10240).ap()
    wgbs = [wg_b0, wg_b1]
    wubs = [wu_b0, wu_b1]
    wg_p0 = alloc("wg_p0", [128, 8, 128], BF16, 768).ap()
    wu_p0 = alloc("wu_p0", [128, 8, 128], BF16, 196160).ap()
    xld2 = alloc("xld2", [128, 1024], F32, ES + 4096).ap()
    outst = alloc("outst", [128, 1024], F32, ES + 8192).ap()
    xld = alloc("xld", [128, 4, 1024], F32, R0 + 0).ap()
    xnch = alloc("xnch", [128, 2, 1024], BF16, R0 + 16384).ap()
    bns = alloc("bns", [128, 2, 6], F32, R0 + 20480).ap()
    mv = alloc("mv", [128, 16, 2], F32, R0 + 20544).ap()
    stats = alloc("stats", [128, 6, 16], F32, R0 + 20672).ap()
    hfsb = alloc("hfsb", [128, 8], F32, R0 + 21056).ap()
    hks = alloc("hks", [128, 9, 128], BF16, R0 + 21088).ap()
    wbuf = alloc("wbuf", [128, 2, 8, 128], BF16, R0 + 0).ap()
    urow0 = alloc("urow0", [128, 2050], F32, R0 + 4096).ap()
    urow1 = alloc("urow1", [128, 2050], F32, R0 + 12320).ap()
    s_a0 = alloc("s_a0", [128, 2048], F32, R0 + 20544).ap()
    s_a1 = alloc("s_a1", [128, 2048], F32, R0 + 135168).ap()
    s_as = [s_a0, s_a1]
    C2 = R0 + 3 * K32
    s_b = alloc("s_b", [128, 2048], F32, C2).ap()
    s2t = alloc("s2t", [128, 2048], F32, C2 + 8192).ap()
    vinbf = alloc("vinbf", [128, 2, 2048], BF16, C2 + 16384).ap()
    skpt = alloc("skpt", [128, 1024], F32, C2).ap()
    vscr = alloc("vscr", [128, 2, 1024], F32, C2 + 4096).ap()

    urows = [urow0, urow1]

    streams = {"g": [], "y": [], "t": [], "s": [], "v": []}

    def em(which, fn):
        streams[which].append(fn)

    with ExitStack() as ctx:
        E = ctx.enter_context
        P01 = E(nc.psum_tensor("P01", [128, 1024], F32)).ap()
        P23 = E(nc.psum_tensor("P23", [128, 1024], F32)).ap()
        P45 = E(nc.psum_tensor("P45", [128, 1024], F32)).ap()
        P67 = E(nc.psum_tensor("P67", [128, 1024], F32)).ap()
        qG = S(E(nc.semaphore("qG")))
        qS = S(E(nc.semaphore("qS")))
        qT = S(E(nc.semaphore("qT")))
        qXA = [S(E(nc.semaphore(f"qXA{i}"))) for i in range(4)]
        qGU = [S(E(nc.semaphore(f"qGU{i}"))) for i in range(2)]
        qP = S(E(nc.semaphore("qP")))
        qWB = [qG, qG]
        qD = qG
        qH = qG
        qX2 = qXA[0]
        pe = S(E(nc.semaphore("pe")))
        dv = S(E(nc.semaphore("dv")))
        ac = S(E(nc.semaphore("ac")))
        gp = S(E(nc.semaphore("gp")))
        blk = E(nc.Block())

        def bf16v(ps, lo, n):
            return ps[:, lo:lo + n].bitcast(BF16)

        def drain(eng, sem, instr):
            sem.inc(instr)
            eng.wait_ge(sem.h, sem.v)

        def fmark(eng, sem, instr):
            sem.inc(instr)
            eng.wait_ge(sem.h, sem.v)
            return sem.inc(eng.nop())

        def W(eng, sem, val):
            if val and val > 0:
                eng.wait_ge(sem.h, val)

        def g_all(g):
            for j in (1, 3):
                qXA[j % 4].inc(g.dma_start(out=xld[:, j % 4, :],
                                           in_=x_d[128 * j:128 * j + 128, :]), 16)
                rec(("qsx", j), qXA[j % 4].v)
            g.memset(ones1, 1.0)
            g.memset(z128, 0.0)
            drain(g, gp, g.memset(identb, 0.0))
            g.affine_select(out=identb, in_=identb, compare_op=ALU.not_equal,
                            fill=1.0, base=0, pattern=[[-1, 128]], channel_multiplier=1)
            drain(g, gp, g.memset(identf, 0.0))
            g.affine_select(out=identf, in_=identf, compare_op=ALU.not_equal,
                            fill=1.0, base=0, pattern=[[-1, 128]], channel_multiplier=1)
            drain(g, gp, g.memset(Jb, 0.0))
            g.memset(z3, 0.0)
            g.memset(eps128, EPS)
            i = g.affine_select(out=Jb, in_=Jb, compare_op=ALU.not_equal,
                                fill=1.0, base=-127, pattern=[[1, 128]], channel_multiplier=1)
            rec(("gp_builds",), gp.inc(i))
            g.dma_start(out=w1s, in_=w1_d).then_inc(qG.h, 16)
            g.dma_start(out=b1c, in_=b1_d.rearrange("(p f) -> p f", f=1)).then_inc(qG.h, 16)
            g.dma_start(out=b2c, in_=b2_d.rearrange("(p f) -> p f", f=1)).then_inc(qG.h, 16)
            g.dma_start(out=w2t, in_=w2_d).then_inc(qG.h, 16)
            g.dma_start(out=w3t, in_=w3_d).then_inc(qG.h, 16)
            g.dma_start(out=fbrow, in_=fb_d.rearrange("(p f) -> p f", p=1)).then_inc(qG.h, 16)
            g.dma_start(out=boutrow, in_=bout_d.rearrange("(p f) -> p f", p=1)).then_inc(qG.h, 16)
            qG.v += 7 * 16
            rec(("qg_hfilt",), qG.v)
            g.wait_ge(qG.h, qG.v)

            for lo, hi in ((0, 8), (8, 16)):
                eo = 0 if lo == 0 else 2048
                g.dma_start(out=bincols[:, lo:hi],
                            in_=b_in_d[eo:eo + 1024].rearrange("(a p) -> p a", p=128),
                            allow_slow_non_contiguous=True).then_inc(qG.h, 16)
                g.dma_start(out=bccols[:, lo:hi],
                            in_=bconv_d[eo:eo + 1024].rearrange("(a p) -> p a", p=128),
                            allow_slow_non_contiguous=True).then_inc(qG.h, 16)
                g.dma_start(out=wtap[:, lo:hi, :],
                            in_=wconv_d.rearrange("(a p) t -> p a t", p=128)[:, eo // 128:eo // 128 + 8, :]).then_inc(qG.h, 16)
                qG.v += 3 * 16
            g.wait_ge(qG.h, qG.v)

            for j in range(5, NB, 2):
                W(g, ac, pm("xnc", j - 4))
                qXA[j % 4].inc(g.dma_start(out=xld[:, j % 4, :],
                                           in_=x_d[128 * j:128 * j + 128, :]), 16)
                rec(("qsx", j), qXA[j % 4].v)

            W(g, ac, pm("ac_hfev"))
            g.dma_start(out=hpad_d[128:1152].rearrange("(a p) -> p a", p=128),
                        in_=hfsb, allow_slow_non_contiguous=True).then_inc(qG.h, 16)
            g.dma_start(out=hpad_d[0:128].rearrange("(p f) -> p f", p=1), in_=z128).then_inc(qG.h, 16)
            g.dma_start(out=hpad_d[1152:1280].rearrange("(p f) -> p f", p=1), in_=z128).then_inc(qG.h, 16)
            qG.v += 3 * 16
            g.wait_ge(qG.h, qG.v)
            rec(("qg_a0",), qG.v)
            hank = bass.AP(tensor=hpad_d.tensor, offset=1, ap=[[1, 128], [128, 9], [1, 128]])
            g.dma_start(out=hks, in_=hank).then_inc(qH.h, 16)
            qH.v += 16
            rec(("qg_hk",), qH.v)
            g.wait_ge(qH.h, qH.v)

            def g_vin(cc2):
                W(g, dv, pm("taps", 2 * cc2 + 1))
                if cc2 >= 2:
                    W(g, pe, pm("vt", cc2 - 2))
                i = g.tensor_tensor(out=vinbf[:, cc2 % 2, :], in0=s2t,
                                    in1=s0_s[:, cc2, :], op=ALU.mult)
                rec(("vin", cc2), fmark(g, gp, i))

            for k in range(16):
                cs, sh = k // 2, k % 2
                eoff = 128 * cs if sh == 0 else 2048 + 128 * cs
                if k < 2:
                    W(g, dv, pm("trev", 15))
                    W(g, dv, pm("st", 15))
                else:
                    W(g, pe, pm("umm", k - 2))
                if k >= 1:
                    W(g, qG, pm("qgw", k - 1))
                qG.inc(g.dma_start(out=wbuf[:, k % 2],
                                   in_=w_in_d.rearrange("(a p) e -> p a e", p=128)[:, :, eoff:eoff + 128]), 16)
                rec(("qgw", k), qG.v)
                if k >= 4 and k % 2 == 0:
                    g_vin((k - 4) // 2)
            for cc2 in (6, 7):
                g_vin(cc2)

            for s in range(2):
                for hc in range(NHC):
                    if s == 0 and hc == 2:
                        W(g, pe, pm("y", 15))
                        qP.inc(g.dma_start(
                            out=wg_p0,
                            in_=wgT_d.rearrange("(a p) h -> p a h", p=128)[:, :, 0:128]), 16)
                        qP.inc(g.dma_start(
                            out=wu_p0,
                            in_=wuT_d.rearrange("(a p) h -> p a h", p=128)[:, :, 0:128]), 16)
                    if s == 1 and hc == 0:
                        continue
                    slot = hc % 2 if s == 0 else (hc + 1) % 2
                    if s == 1 and hc == 1:
                        W(g, pe, pm("gu", 0, 22))
                    if s == 1 and hc == 2:
                        W(g, pe, pm("gu", 0, 23))
                        W(g, qT, pm("qtst", 0, 7))
                    if (s == 0 and hc >= 2) or (s == 1 and hc >= 3):
                        W(g, pe, pm("gu", s, hc - 2))
                    qGU[slot].inc(g.dma_start(
                        out=wgbs[slot],
                        in_=wgT_d.rearrange("(a p) h -> p a h", p=128)[:, :, 128 * hc:128 * hc + 128]), 16)
                    qGU[slot].inc(g.dma_start(
                        out=wubs[slot],
                        in_=wuT_d.rearrange("(a p) h -> p a h", p=128)[:, :, 128 * hc:128 * hc + 128]), 16)
                    rec(("qggu", s, hc), qGU[slot].v)

                    if s == 0 and 4 <= hc <= 19:
                        if hc == 4:
                            W(g, pe, pm("vtr", 15))
                            W(g, dv, pm("gate", 15))
                        if hc < 16:
                            c2 = 2 * (hc - 4)
                            qD.inc(g.dma_start(
                                out=wdb[:, c2:c2 + 2, :],
                                in_=wdT_d.rearrange("(a p) d -> p a d", p=128)[:, c2:c2 + 2, :]), 16)
                        else:
                            c2 = 2 * (hc - 16)
                            qD.inc(g.dma_start(
                                out=woutb[:, c2:c2 + 2, :],
                                in_=wout_d.rearrange("(a p) d -> p a d", p=128)[:, c2:c2 + 2, :]), 16)
                        rec(("qgwd",), qD.v)
        em("g", g_all)

        def y_all(sy):
            for j in range(0, NB, 2):
                if j >= 4:
                    W(sy, ac, pm("xnc", j - 4))
                qXA[j % 4].inc(sy.dma_start(out=xld[:, j % 4, :],
                                            in_=x_d[128 * j:128 * j + 128, :]), 16)
                rec(("qsx", j), qXA[j % 4].v)

            if DEBUG:
                W(sy, ac, pm("a0ev"))
                qT.inc(sy.dma_start(out=dbg["dbg_hf"], in_=hfsb), 16)
                qT.inc(sy.dma_start(out=dbg["dbg_H"], in_=Hs), 16)
                qT.inc(sy.dma_start(out=dbg["dbg_fbrep"], in_=fbrep), 16)
                W(sy, dv, pm("trev", 15))
                qT.inc(sy.dma_start(out=dbg["dbg_xnT"], in_=xnT), 16)
                qT.inc(sy.dma_start(out=dbg["dbg_stats"], in_=stats), 16)
                W(sy, ac, pm("uev", 0))
                qT.inc(sy.dma_start(out=dbg["dbg_u0"], in_=urow0), 16)
                W(sy, dv, pm("taps", 14))
                qT.inc(sy.dma_start(out=dbg["dbg_s0"], in_=s0_s), 16)
                W(sy, ac, pm("vtev", 7))
                qT.inc(sy.dma_start(out=dbg["dbg_vinT"], in_=vinT), 16)
                W(sy, dv, pm("gate", 15))
                qT.inc(sy.dma_start(out=dbg["dbg_v2"], in_=v2_s), 16)
                W(sy, dv, pm("h", 0, 23))
                qT.inc(sy.dma_start(out=dbg["dbg_h0"], in_=h_buf), 16)

            for s in range(2):
                for jl in range(8):
                    j = 8 * s + jl
                    if jl == 0:
                        W(sy, dv, pm("h", s, 23))
                    if jl >= 1 or s == 1:
                        W(sy, dv, pm("oev", s, jl - 1) if jl >= 1 else pm("oev", 0, 7))
                    qX2.inc(sy.dma_start(out=xld2, in_=x_d[128 * j:128 * j + 128, :]), 16)
                    rec(("qsx2", s, jl), qX2.v)
                    W(sy, dv, pm("oev", s, jl))
                    qT.inc(sy.dma_start(out=out_d[128 * j:128 * j + 128, :], in_=outst), 16)
                    rec(("qtst", s, jl), qT.v)
            sy.wait_ge(qT.h, qT.v)
        em("y", y_all)

        def t_all(t):
            W(t, qG, pm("qg_hfilt"))
            W(t, gp, pm("gp_builds"))
            rec(("pe_w1",), pe.inc(t.matmul(P01[0:64, 1:2], w1s, z3, start=True, stop=True)))
            W(t, ac, pm("ac_h1"))
            rec(("pe_ps1",), pe.inc(t.matmul(P01[0:64, 0:1], w2t, h1c, start=True, stop=True)))
            W(t, ac, pm("ac_h2"))
            for j in range(8):
                i = t.matmul(P23[:, j:j + 1], w3t[:, 128 * j:128 * j + 128], b1c,
                             start=True, stop=True)
            rec(("pe_hf",), pe.inc(i))

            for j in range(NB):
                W(t, ac, pm("xnc", j))
                if j == 0:
                    W(t, ac, pm("ac_h2"))
                elif j == 1:
                    W(t, ac, pm("ac_hfev"))
                else:
                    W(t, ac if j % 2 == 0 else dv, pm("trev", j - 2))
                ps = P01 if j % 2 == 0 else P23
                view = bf16v(ps, 0, 512).rearrange("p (a f) -> p a f", f=128)
                for a in range(8):
                    i = t.transpose(view[:, a, :], xnch[:, j % 2, 128 * a:128 * a + 128], identb)
                rec(("tr", j), pe.inc(i))

            W(t, qH, pm("qg_hk"))
            W(t, dv, pm("trev", 15))
            for d in range(8):
                t.matmul(P01[:, 128 * d:128 * d + 128], Jb, hks[:, d, :], start=True, stop=True)
            t.matmul(P23[:, 0:128], Jb, hks[:, 8, :], start=True, stop=True)
            t.matmul(P45[:, 0:512], ones1, fbrow[:, 0:512], start=True, stop=True)
            i = t.matmul(P45[:, 512:1024], ones1, fbrow[:, 512:1024], start=True, stop=True)
            rec(("pe_a0",), pe.inc(i))

            for it in range(18):
                if it < 16:
                    k = it
                    cs, sh = k // 2, k % 2
                    W(t, qWB[k % 2], pm("qgw", k))
                    if k < 2:
                        W(t, ac, pm("a0ev"))
                    else:
                        W(t, ac, pm("uev", k - 2))
                    if sh == 1 and cs >= 3:
                        W(t, ac, pm("vtev", cs - 3))
                    pair = (P01, P23) if k % 2 == 0 else (P45, P67)
                    tiles = [pair[0][:, 0:512], pair[0][:, 512:1024],
                             pair[1][:, 0:512], pair[1][:, 512:1024]]
                    for a in range(8):
                        for tc in range(4):
                            i = t.matmul(tiles[tc], wbuf[:, k % 2, a, :],
                                         xnT[:, a, 512 * tc:512 * tc + 512],
                                         start=(a == 0), stop=(a == 7))
                    rec(("umm", k), pe.inc(i))
                if (it >= 6 and it % 2 == 0) or it == 17:
                    for cc in ([(it - 6) // 2] if it < 17 else [6, 7]):
                        W(t, gp, pm("vin", cc))
                        if cc >= 1:
                            W(t, ac, pm("vtev", cc - 1))
                        W(t, ac, pm("uev", min(2 * cc + 5, 15)))
                        view = bf16v(P45, 0, 1024).rearrange("p (a f) -> p a f", f=128)
                        for jt in range(16):
                            i = t.transpose(view[:, jt, :],
                                            vinbf[:, cc % 2, 128 * jt:128 * jt + 128], identb)
                        rec(("vt", cc), pe.inc(i))

            for j in range(NB + 1):
                if j < NB:
                    if j < 2:
                        W(t, ac, pm("uev", 14 + j))
                        W(t, ac, pm("vtev", 7))
                    else:
                        W(t, dv, pm("vs", j - 2))
                    psy = P01 if j % 2 == 0 else P23
                    dmax = min(j, 8)
                    for n2 in range(2):
                        for d in range(dmax + 1):
                            i = t.matmul(psy[:, 512 * n2:512 * n2 + 512], Hs[:, d, :],
                                         vinT[:, j - d, 512 * n2:512 * n2 + 512],
                                         start=(d == 0), stop=(d == dmax))
                    rec(("y", j), pe.inc(i))
                if j >= 1:
                    jj = j - 1
                    W(t, dv, pm("vs", jj))
                    if jj >= 2:
                        W(t, dv, pm("gate", jj - 2))
                    else:
                        W(t, ac, pm("uev", 15))
                        W(t, ac, pm("vtev", 7))
                    pt = P45 if jj % 2 == 0 else P67
                    ptv = pt.rearrange("p (a f) -> p a f", f=128)
                    for q in range(8):
                        i = t.transpose(ptv[:, q, :], vscr[:, jj % 2, 128 * q:128 * q + 128], identf)
                    rec(("vtr", jj), pe.inc(i))

            for s in range(2):
                scol = 1024 * s
                for hc in range(NHC):
                    if s == 1 and hc == 0:
                        W(t, qP, 32)
                        wg_src, wu_src = wg_p0, wu_p0
                    else:
                        slot = hc % 2 if s == 0 else (hc + 1) % 2
                        W(t, qGU[slot], pm("qggu", s, hc))
                        wg_src, wu_src = wgbs[slot], wubs[slot]
                    if hc >= 2:
                        W(t, ac, pm("sil", s, hc - 2))
                        W(t, dv, pm("h", s, hc - 2))
                    elif s == 0:
                        W(t, dv, pm("gate", 15))
                    else:
                        W(t, dv, pm("oev", 0, 7))
                        W(t, ac, pm("sil", 0, 22 + hc))
                        W(t, dv, pm("h", 0, 22 + hc))
                    psg = P01 if hc % 2 == 0 else P45
                    psu = P23 if hc % 2 == 0 else P67
                    for a in range(8):
                        for n2 in range(2):
                            t.matmul(psg[:, 512 * n2:512 * n2 + 512], wg_src[:, a, :],
                                     xnT[:, a, scol + 512 * n2:scol + 512 * n2 + 512],
                                     start=(a == 0), stop=(a == 7))
                    for a in range(8):
                        for n2 in range(2):
                            i = t.matmul(psu[:, 512 * n2:512 * n2 + 512], wu_src[:, a, :],
                                         xnT[:, a, scol + 512 * n2:scol + 512 * n2 + 512],
                                         start=(a == 0), stop=(a == 7))
                    rec(("gu", s, hc), pe.inc(i))

                for jl in range(8):
                    j = 8 * s + jl
                    W(t, qD, pm("qgwd"))
                    if jl >= 2:
                        W(t, dv, pm("oev", s, jl - 2))
                    else:
                        W(t, ac, pm("sil", s, 23))
                        W(t, dv, pm("h", s, 23))
                    pso = P01 if jl % 2 == 0 else P23
                    first = [True, True]
                    for cc in range(8):
                        for n2 in range(2):
                            t.matmul(pso[:, 512 * n2:512 * n2 + 512],
                                     v2_s[:, cc, 128 * j:128 * j + 128],
                                     woutb[:, cc, 512 * n2:512 * n2 + 512],
                                     start=first[n2], stop=False)
                            first[n2] = False
                    for n2 in range(2):
                        t.matmul(pso[:, 512 * n2:512 * n2 + 512], ones1,
                                 boutrow[:, 512 * n2:512 * n2 + 512], start=False, stop=False)
                    for hc in range(NHC):
                        for n2 in range(2):
                            i = t.matmul(pso[:, 512 * n2:512 * n2 + 512],
                                         h_buf[:, hc, 128 * jl:128 * jl + 128],
                                         wdb[:, hc, 512 * n2:512 * n2 + 512],
                                         start=False, stop=(hc == NHC - 1))
                    rec(("guo", s, jl), pe.inc(i))
        em("t", t_all)

        def s_all(s):
            W(s, qG, pm("qg_hfilt"))
            W(s, pe, pm("pe_w1"))
            rec(("ac_h1",), fmark(s, ac, s.activation(out=h1c, in_=P01[0:64, 1:2], func=AF.Silu,
                                                       bias=b1c, scale=1.0)))
            W(s, pe, pm("pe_ps1"))
            rec(("ac_h2",), fmark(s, ac, s.activation(out=b1c, in_=P01[0:64, 0:1], func=AF.Silu,
                                                       bias=b2c, scale=1.0)))
            W(s, pe, pm("pe_hf"))
            rec(("ac_hfev",), fmark(s, ac, s.copy(hfsb, P23[:, 0:8])))

            for it in range(NB + 2):
                if it < NB:
                    j = it
                    W(s, dv, pm("st", j))
                    if j == 0:
                        W(s, gp, pm("gp_builds"))
                    rec(("sd", j), fmark(s, ac, s.activation(out=stats[:, 5, j:j + 1],
                                                              in_=mv[:, j, 1:2],
                                                              func=AF.Sqrt, bias=EPS)))
                if 1 <= it <= NB:
                    j = it - 1
                    W(s, dv, pm("mrs", j))
                    if j >= 2:
                        W(s, pe, pm("tr", j - 2))
                    i = s.activation(out=xnch[:, j % 2, :], in_=xld[:, j % 4, :],
                                     func=AF.Identity, scale=stats[:, 4, j:j + 1],
                                     bias=stats[:, 3, j:j + 1])
                    rec(("xnc", j), ac.inc(i))
                if it >= 2 and (it - 2) % 2 == 0:
                    j = it - 2
                    W(s, pe, pm("tr", j))
                    i = s.copy(xnT[:, :, 128 * j:128 * j + 128],
                               bf16v(P01, 0, 512).rearrange("p (a f) -> p a f", f=128))
                    rec(("trev", j), ac.inc(i))

            W(s, pe, pm("pe_a0"))
            s.copy(Hs[:, 0:8, :], bf16v(P01, 0, 512).rearrange("p (a f) -> p a f", f=128))
            s.copy(Hs[:, 8, :], bf16v(P23, 0, 64))
            rec(("a0ev",), ac.inc(s.copy(fbrep, P45)))

            for k in range(16):
                cs, sh = k // 2, k % 2
                cidx = cs + 8 * sh
                if k == 0:
                    W(s, qG, pm("qg_a0"))
                W(s, pe, pm("umm", k))
                if k >= 2:
                    W(s, dv, pm("taps", k - 2))
                pair = (P01, P23) if k % 2 == 0 else (P45, P67)
                ur = urows[k % 2]
                s.activation(out=ur[:, 2:1026], in_=pair[0],
                             func=AF.Identity, bias=bincols[:, cidx:cidx + 1], scale=1.0)
                i = s.activation(out=ur[:, 1026:2050], in_=pair[1],
                                 func=AF.Identity, bias=bincols[:, cidx:cidx + 1], scale=1.0)
                drain(s, ac, i)
                i = s.activation(out=s_as[k % 2], in_=ur[:, 2:2050], func=AF.Identity,
                                 bias=bccols[:, cidx:cidx + 1], scale=wtap[:, cidx, 2:3])
                rec(("uev", k), fmark(s, ac, i))
                if (k >= 6 and k % 2 == 0) or k == 15:
                    for cc in ([(k - 6) // 2] if k < 15 else [5, 6, 7]):
                        W(s, pe, pm("vt", cc))
                        s.copy(vinT[:, 0:8, 128 * cc:128 * cc + 128],
                               bf16v(P45, 0, 512).rearrange("p (a f) -> p a f", f=128))
                        i = s.copy(vinT[:, 8:16, 128 * cc:128 * cc + 128],
                                   bf16v(P45, 512, 512).rearrange("p (a f) -> p a f", f=128))
                        rec(("vtev", cc), ac.inc(i))

            for s2 in range(2):
                for hc in range(NHC):
                    W(s, pe, pm("gu", s2, hc))
                    if hc >= 1:
                        W(s, dv, pm("h", s2, hc - 1))
                    elif s2 == 1:
                        W(s, dv, pm("oev", 0, 7))
                    psg = P01 if hc % 2 == 0 else P45
                    rec(("sil", s2, hc), ac.inc(
                        s.activation(out=sil_s, in_=psg, func=AF.Silu, scale=SC127)))
        em("s", s_all)

        def v_all(v):
            for it in range(NB + 2):
                if it < NB:
                    j = it
                    W(v, qXA[j % 4], pm("qsx", j))
                    v.bn_stats(bns[:, 0, :], xld[:, j % 4, 0:512])
                    drain(v, dv, v.bn_stats(bns[:, 1, :], xld[:, j % 4, 512:1024]))
                    rec(("st", j), fmark(v, dv, v.bn_aggr(mv[:, j, :], bns.rearrange("p c s -> p (c s)"))))
                if 1 <= it <= NB:
                    j = it - 1
                    W(v, ac, pm("sd", j))
                    drain(v, dv, v.reciprocal(stats[:, 4, j:j + 1], stats[:, 5, j:j + 1]))
                    i = v.tensor_scalar(out=stats[:, 3, j:j + 1], in0=mv[:, j, 0:1],
                                        scalar1=stats[:, 4, j:j + 1], scalar2=-1.0,
                                        op0=ALU.mult, op1=ALU.mult)
                    rec(("mrs", j), fmark(v, dv, i))
                if it >= 2 and (it - 2) % 2 == 1:
                    j = it - 2
                    W(v, pe, pm("tr", j))
                    i = v.tensor_copy(xnT[:, :, 128 * j:128 * j + 128],
                                      bf16v(P23, 0, 512).rearrange("p (a f) -> p a f", f=128))
                    rec(("trev", j), dv.inc(i))

            W(v, pe, pm("tr", 15))
            v.memset(urow0[:, 0:2], 0.0)
            drain(v, dv, v.memset(urow1[:, 0:2], 0.0))
            for it in range(18):
                if 1 <= it <= 16:
                    kk = it - 1
                    csk, shk = kk // 2, kk % 2
                    cidxk = csk + 8 * shk
                    if kk == 0:
                        W(v, qG, pm("qg_a0"))
                    W(v, ac, pm("uev", kk))
                    drain(v, dv, v.scalar_tensor_tensor(out=s_b, in0=urows[kk % 2][:, 1:2049],
                                                        scalar=wtap[:, cidxk, 1:2],
                                                        in1=s_as[kk % 2], op0=ALU.mult, op1=ALU.add))
                    if shk == 1 and csk >= 1:
                        W(v, gp, pm("vin", csk - 1))
                    dest = s0_s[:, csk, :] if shk == 0 else s2t
                    i = v.scalar_tensor_tensor(out=dest, in0=urows[kk % 2][:, 0:2048],
                                               scalar=wtap[:, cidxk, 0:1],
                                               in1=s_b, op0=ALU.mult, op1=ALU.add)
                    if shk == 0:
                        rec(("taps", kk), dv.inc(i))
                        v.wait_ge(dv.h, dv.v)
                    else:
                        rec(("taps", kk), fmark(v, dv, i))

            for j in range(NB + 1):
                if j < NB:
                    W(v, pe, pm("y", j))
                    drain(v, dv, v.tensor_tensor(out=skpt, in0=vinT[:, j, :], in1=fbrep,
                                                 op=ALU.mult))
                    if j >= 2:
                        W(v, pe, pm("vtr", j - 2))
                    psy = P01 if j % 2 == 0 else P23
                    i = v.tensor_tensor(out=vscr[:, j % 2, :], in0=skpt, in1=psy, op=ALU.add)
                    rec(("vs", j), fmark(v, dv, i))
                if j >= 1:
                    jj = j - 1
                    W(v, pe, pm("vtr", jj))
                    pt = P45 if jj % 2 == 0 else P67
                    for h2 in range(2):
                        i = v.tensor_tensor(
                            out=v2_s[:, 4 * h2:4 * h2 + 4, 128 * jj:128 * jj + 128],
                            in0=pt[:, 512 * h2:512 * h2 + 512].rearrange("p (a f) -> p a f", f=128),
                            in1=s0_s[:, 4 * h2:4 * h2 + 4, 128 * jj:128 * jj + 128],
                            op=ALU.mult)
                    rec(("gate", jj), dv.inc(i))

            for s in range(2):
                for hc in range(NHC):
                    W(v, ac, pm("sil", s, hc))
                    psu = P23 if hc % 2 == 0 else P67
                    i = v.scalar_tensor_tensor(out=h_buf[:, hc, :], in0=psu, scalar=SC127SQ,
                                               in1=sil_s, op0=ALU.mult, op1=ALU.mult)
                    rec(("h", s, hc), dv.inc(i))
                for jl in range(8):
                    W(v, pe, pm("guo", s, jl))
                    W(v, qX2, pm("qsx2", s, jl))
                    if jl >= 1 or s == 1:
                        W(v, qT, pm("qtst", s, jl - 1) if jl >= 1 else pm("qtst", 0, 7))
                    pso = P01 if jl % 2 == 0 else P23
                    i = v.tensor_tensor(out=outst, in0=pso, in1=xld2, op=ALU.add)
                    rec(("oev", s, jl), fmark(v, dv, i))
        em("v", v_all)

        blk.gpsimd(lambda g: [f(g) for f in streams["g"]])
        blk.sync(lambda sy: [f(sy) for f in streams["y"]])
        blk.tensor(lambda t: [f(t) for f in streams["t"]])
        blk.scalar(lambda s: [f(s) for f in streams["s"]])
        blk.vector(lambda v: [f(v) for f in streams["v"]])

    return nc, marks


def kernel(**inputs):
    f32 = np.float32
    x = np.ascontiguousarray(np.asarray(inputs["L0_attn_x"], f32))
    w_in = np.ascontiguousarray(np.asarray(inputs["L0_attn_w_in"], f32))
    b_in = np.ascontiguousarray(np.asarray(inputs["L0_attn_b_in"], f32))
    w_conv = np.ascontiguousarray(np.asarray(inputs["L0_attn_w_conv"], f32).reshape(3072, 3))
    b_conv = np.ascontiguousarray(np.asarray(inputs["L0_attn_b_conv"], f32))
    w1 = np.ascontiguousarray(np.asarray(inputs["L0_attn_w1"], f32))
    b1 = np.ascontiguousarray(np.asarray(inputs["L0_attn_b1"], f32))
    w2 = np.ascontiguousarray(np.asarray(inputs["L0_attn_w2"], f32))
    b2 = np.ascontiguousarray(np.asarray(inputs["L0_attn_b2"], f32))
    w3 = np.ascontiguousarray(np.asarray(inputs["L0_attn_w3"], f32))
    fb = np.ascontiguousarray(np.asarray(inputs["L0_attn_filt_bias"], f32))
    w_out = np.ascontiguousarray(np.asarray(inputs["L0_attn_w_out"], f32))
    b_out = np.ascontiguousarray(np.asarray(inputs["L0_attn_b_out"], f32))
    wgT = np.ascontiguousarray(np.asarray(inputs["L0_ffn_w_gate"], np.int32).T)
    wuT = np.ascontiguousarray(np.asarray(inputs["L0_ffn_w_up"], np.int32).T)
    wdT = np.ascontiguousarray(np.asarray(inputs["L0_ffn_w_down"], np.int32).T)

    _, m1 = build_nc(None)
    nc, _ = build_nc(m1)
    in_maps = []
    for b in range(8):
        in_maps.append(dict(x=np.ascontiguousarray(x[b]), w_in=w_in, b_in=b_in,
                            w_conv=w_conv, b_conv=b_conv, w1=w1, b1=b1, w2=w2, b2=b2, w3=w3,
                            filt_bias=fb, w_out=w_out, b_out=b_out,
                            wgT=wgT, wuT=wuT, wdT=wdT))
    res = run_bass_kernel_spmd(nc, in_maps, core_ids=list(range(8)))
    return np.stack([np.asarray(r["out"], np.float32) for r in res.results], axis=0)


if __name__ == "__main__":
    import reference as R
    import jax
    with jax.default_device(jax.devices("cpu")[0]):
        inp = {k: np.asarray(v) for k, v in R.setup_inputs(0).items()}
    out = kernel(**inp)
    print("out", out.shape, out.dtype)


# version 25 (final)
# speedup vs baseline: 1.0185x; 1.0062x over previous
import numpy as np
import concourse.bass as bass
import concourse.mybir as mybir
from concourse import bass_utils as _bu
from concourse.bass_utils import run_bass_kernel_spmd
from contextlib import ExitStack

if not getattr(_bu, "_ldw_patched", False):
    _orig_run_command = _bu.run_command

    def _run_command_ldw(argv, **kw):
        argv = ["--enable-ldw-opt=true" if a == "--enable-ldw-opt=false" else a
                for a in argv]
        return _orig_run_command(argv, **kw)

    _bu.run_command = _run_command_ldw
    _bu._ldw_patched = True

F32 = mybir.dt.float32
BF16 = mybir.dt.bfloat16
I32 = mybir.dt.int32
U8 = mybir.dt.uint8
AF = mybir.ActivationFunctionType
ALU = mybir.AluOpType
AX = mybir.AxisListType

DEBUG = False
T = 2048
D = 1024
NB = 16
NHC = 24
EPS = 1e-5
SC127 = 1.0 / 127.0
SC127SQ = 1.0 / 16129.0


class Arena:
    def __init__(self, nc, name, total):
        self.nc = nc
        self.handle = nc.alloc_sbuf_tensor(name, [128, total], U8)
        self.base = nc.lookup_mloc(self.handle).addr
        self.total = total

    def at(self, name, shape, dtype, off):
        per_part = 1
        for s in shape[1:]:
            per_part *= s
        per_part *= mybir.dt.size(dtype)
        assert off % 32 == 0, (name, off)
        assert off + per_part <= self.total, (name, off, per_part, self.total)
        h = self.nc.virtual_sbuf_tensor(name, shape, dtype)
        self.nc._tensor(h.name, shape, dtype, type="SB")
        mloc = self.nc.lookup_mloc(h)
        mloc.allocated = True
        mloc.addr = self.base + off
        return h


class S:
    def __init__(self, h):
        self.h = h
        self.v = 0

    def inc(self, instr, n=1):
        instr.then_inc(self.h, n)
        self.v += n
        return self.v


def build_nc(prev=None):
    marks = {}

    def rec(key, val):
        marks[key] = val
        return val

    P = prev or {}

    def pm(*key):
        return P.get(key, 0)

    nc = bass.Bass("TRN2", target_bir_lowering=False)

    x_d = nc.dram_tensor("x", [T, D], F32, kind="ExternalInput").ap()
    w_in_d = nc.dram_tensor("w_in", [D, 3 * D], F32, kind="ExternalInput").ap()
    b_in_d = nc.dram_tensor("b_in", [3 * D], F32, kind="ExternalInput").ap()
    wconv_d = nc.dram_tensor("w_conv", [3 * D, 3], F32, kind="ExternalInput").ap()
    bconv_d = nc.dram_tensor("b_conv", [3 * D], F32, kind="ExternalInput").ap()
    w1_d = nc.dram_tensor("w1", [3, 64], F32, kind="ExternalInput").ap()
    b1_d = nc.dram_tensor("b1", [64], F32, kind="ExternalInput").ap()
    w2_d = nc.dram_tensor("w2", [64, 64], F32, kind="ExternalInput").ap()
    b2_d = nc.dram_tensor("b2", [64], F32, kind="ExternalInput").ap()
    w3_d = nc.dram_tensor("w3", [64, D], F32, kind="ExternalInput").ap()
    fb_d = nc.dram_tensor("filt_bias", [D], F32, kind="ExternalInput").ap()
    wout_d = nc.dram_tensor("w_out", [D, D], F32, kind="ExternalInput").ap()
    bout_d = nc.dram_tensor("b_out", [D], F32, kind="ExternalInput").ap()
    wgT_d = nc.dram_tensor("wgT", [D, 3 * D], I32, kind="ExternalInput").ap()
    wuT_d = nc.dram_tensor("wuT", [D, 3 * D], I32, kind="ExternalInput").ap()
    wdT_d = nc.dram_tensor("wdT", [3 * D, D], I32, kind="ExternalInput").ap()
    out_d = nc.dram_tensor("out", [T, D], F32, kind="ExternalOutput").ap()
    hpad_d = nc.dram_tensor("hpad", [1280], F32, kind="Internal").ap()
    if DEBUG:
        dbg = {
            "dbg_hf": nc.dram_tensor("dbg_hf", [128, 8], F32, kind="ExternalOutput").ap(),
            "dbg_H": nc.dram_tensor("dbg_H", [128, 9, 128], BF16, kind="ExternalOutput").ap(),
            "dbg_fbrep": nc.dram_tensor("dbg_fbrep", [128, 1024], F32, kind="ExternalOutput").ap(),
            "dbg_xnT": nc.dram_tensor("dbg_xnT", [128, 8, T], BF16, kind="ExternalOutput").ap(),
            "dbg_s0": nc.dram_tensor("dbg_s0", [128, 8, T], BF16, kind="ExternalOutput").ap(),
            "dbg_vinT": nc.dram_tensor("dbg_vinT", [128, 16, 1024], BF16, kind="ExternalOutput").ap(),
            "dbg_v2": nc.dram_tensor("dbg_v2", [128, 8, T], BF16, kind="ExternalOutput").ap(),
            "dbg_h0": nc.dram_tensor("dbg_h0", [128, 24, 1024], BF16, kind="ExternalOutput").ap(),
            "dbg_u0": nc.dram_tensor("dbg_u0", [128, 2050], F32, kind="ExternalOutput").ap(),
            "dbg_s0c0": nc.dram_tensor("dbg_s0c0", [128, 2048], F32, kind="ExternalOutput").ap(),
            "dbg_stats": nc.dram_tensor("dbg_stats", [128, 6, 16], F32, kind="ExternalOutput").ap(),
        }

    ones1 = nc.alloc_sbuf_tensor("ones1", [1, 128], F32).ap()
    z128 = nc.alloc_sbuf_tensor("z128", [1, 128], F32).ap()
    fbrow = nc.alloc_sbuf_tensor("fbrow", [1, 1024], F32).ap()
    boutrow = nc.alloc_sbuf_tensor("boutrow", [1, 1024], F32).ap()
    b1c = nc.alloc_sbuf_tensor("b1c", [64, 1], F32).ap()
    b2c = nc.alloc_sbuf_tensor("b2c", [64, 1], F32).ap()
    h1c = nc.alloc_sbuf_tensor("h1c", [64, 1], F32).ap()
    w1s = nc.alloc_sbuf_tensor("w1s", [3, 64], F32).ap()
    z3 = nc.alloc_sbuf_tensor("z3", [3, 1], F32).ap()
    w2t = nc.alloc_sbuf_tensor("w2t", [64, 64], F32).ap()
    w3t = nc.alloc_sbuf_tensor("w3t", [64, 1024], F32).ap()
    eps128 = nc.alloc_sbuf_tensor("eps128", [128, 1], F32).ap()
    nc.const_aps.register(F32, EPS, eps128)

    ar = Arena(nc, "arena", 198208)

    def alloc(name, shape, dtype, at):
        return ar.at(name, shape, dtype, at)

    identb = alloc("identb", [128, 128], BF16, 0).ap()
    identf = alloc("identf", [128, 128], F32, 256).ap()
    Jb = alloc("Jb", [128, 128], BF16, 768).ap()
    Hs = alloc("Hs", [128, 9, 128], BF16, 1024).ap()
    bincols = alloc("bincols", [128, 16], F32, 3328).ap()
    bccols = alloc("bccols", [128, 16], F32, 3392).ap()
    wtap = alloc("wtap", [128, 16, 3], F32, 3456).ap()
    CONST_END = 3648

    xnT = alloc("xnT", [128, 8, T], BF16, CONST_END).ap()
    R0 = CONST_END + 32768

    K32 = 32768
    v2_s = alloc("v2", [128, 8, T], BF16, R0 + 0).ap()
    s0_s = alloc("s0", [128, 8, T], BF16, R0 + K32).ap()
    vinT = alloc("vinT", [128, 16, 1024], BF16, R0 + 2 * K32).ap()
    h_buf = alloc("h_buf", [128, 24, 1024], BF16, R0 + K32).ap()
    woutb = alloc("woutb", [128, 8, 1024], BF16, R0 + K32 + 49152).ap()
    wdb = alloc("wdb", [128, 24, 1024], BF16, R0 + 3 * K32).ap()
    fbrep = alloc("fbrep", [128, 1024], F32, R0 + 131072).ap()
    ES = R0 + 147456
    wg_b0 = alloc("wg_b0", [128, 8, 128], BF16, ES).ap()
    wu_b0 = alloc("wu_b0", [128, 8, 128], BF16, ES + 2048).ap()
    sil_s = alloc("sil", [128, 1024], F32, ES + 4096).ap()
    wg_b1 = alloc("wg_b1", [128, 8, 128], BF16, ES + 8192).ap()
    wu_b1 = alloc("wu_b1", [128, 8, 128], BF16, ES + 10240).ap()
    wgbs = [wg_b0, wg_b1]
    wubs = [wu_b0, wu_b1]
    wg_p0 = alloc("wg_p0", [128, 8, 128], BF16, 768).ap()
    wu_p0 = alloc("wu_p0", [128, 8, 128], BF16, 196160).ap()
    xld2 = alloc("xld2", [128, 1024], F32, ES + 4096).ap()
    outst = alloc("outst", [128, 1024], F32, ES + 8192).ap()
    xld = alloc("xld", [128, 4, 1024], F32, R0 + 0).ap()
    xnch = alloc("xnch", [128, 2, 1024], BF16, R0 + 16384).ap()
    bns = alloc("bns", [128, 2, 6], F32, R0 + 20480).ap()
    mv = alloc("mv", [128, 16, 2], F32, R0 + 20544).ap()
    stats = alloc("stats", [128, 6, 16], F32, R0 + 20672).ap()
    hfsb = alloc("hfsb", [128, 8], F32, R0 + 21056).ap()
    hks = alloc("hks", [128, 9, 128], BF16, R0 + 21088).ap()
    wbuf = alloc("wbuf", [128, 2, 8, 128], BF16, R0 + 0).ap()
    urow0 = alloc("urow0", [128, 2050], F32, R0 + 4096).ap()
    urow1 = alloc("urow1", [128, 2050], F32, R0 + 12320).ap()
    s_a0 = alloc("s_a0", [128, 2048], F32, R0 + 20544).ap()
    s_a1 = alloc("s_a1", [128, 2048], F32, R0 + 135168).ap()
    s_as = [s_a0, s_a1]
    C2 = R0 + 3 * K32
    s_b = alloc("s_b", [128, 2048], F32, C2).ap()
    s2t = alloc("s2t", [128, 2048], F32, C2 + 8192).ap()
    vinbf = alloc("vinbf", [128, 2, 2048], BF16, C2 + 16384).ap()
    skpt = alloc("skpt", [128, 1024], F32, C2).ap()
    vscr = alloc("vscr", [128, 2, 1024], F32, C2 + 4096).ap()

    urows = [urow0, urow1]

    streams = {"g": [], "y": [], "t": [], "s": [], "v": []}

    def em(which, fn):
        streams[which].append(fn)

    with ExitStack() as ctx:
        E = ctx.enter_context
        P01 = E(nc.psum_tensor("P01", [128, 1024], F32)).ap()
        P23 = E(nc.psum_tensor("P23", [128, 1024], F32)).ap()
        P45 = E(nc.psum_tensor("P45", [128, 1024], F32)).ap()
        P67 = E(nc.psum_tensor("P67", [128, 1024], F32)).ap()
        qG = S(E(nc.semaphore("qG")))
        qS = S(E(nc.semaphore("qS")))
        qT = S(E(nc.semaphore("qT")))
        qXA = [S(E(nc.semaphore(f"qXA{i}"))) for i in range(4)]
        qGU = [S(E(nc.semaphore(f"qGU{i}"))) for i in range(2)]
        qP = S(E(nc.semaphore("qP")))
        qWB = [qG, qG]
        qD = qG
        qH = qG
        qX2 = qXA[0]
        pe = S(E(nc.semaphore("pe")))
        dv = S(E(nc.semaphore("dv")))
        ac = S(E(nc.semaphore("ac")))
        gp = S(E(nc.semaphore("gp")))
        blk = E(nc.Block())

        def bf16v(ps, lo, n):
            return ps[:, lo:lo + n].bitcast(BF16)

        def drain(eng, sem, instr):
            sem.inc(instr)
            eng.wait_ge(sem.h, sem.v)

        def fmark(eng, sem, instr):
            sem.inc(instr)
            eng.wait_ge(sem.h, sem.v)
            return sem.inc(eng.nop())

        def W(eng, sem, val):
            if val and val > 0:
                eng.wait_ge(sem.h, val)

        def g_all(g):
            for j in (1, 3):
                qXA[j % 4].inc(g.dma_start(out=xld[:, j % 4, :],
                                           in_=x_d[128 * j:128 * j + 128, :]), 16)
                rec(("qsx", j), qXA[j % 4].v)
            g.memset(ones1, 1.0)
            g.memset(z128, 0.0)
            drain(g, gp, g.memset(identb, 0.0))
            g.affine_select(out=identb, in_=identb, compare_op=ALU.not_equal,
                            fill=1.0, base=0, pattern=[[-1, 128]], channel_multiplier=1)
            drain(g, gp, g.memset(identf, 0.0))
            g.affine_select(out=identf, in_=identf, compare_op=ALU.not_equal,
                            fill=1.0, base=0, pattern=[[-1, 128]], channel_multiplier=1)
            drain(g, gp, g.memset(Jb, 0.0))
            g.memset(z3, 0.0)
            g.memset(eps128, EPS)
            i = g.affine_select(out=Jb, in_=Jb, compare_op=ALU.not_equal,
                                fill=1.0, base=-127, pattern=[[1, 128]], channel_multiplier=1)
            rec(("gp_builds",), gp.inc(i))
            g.dma_start(out=w1s, in_=w1_d).then_inc(qG.h, 16)
            g.dma_start(out=b1c, in_=b1_d.rearrange("(p f) -> p f", f=1)).then_inc(qG.h, 16)
            g.dma_start(out=b2c, in_=b2_d.rearrange("(p f) -> p f", f=1)).then_inc(qG.h, 16)
            g.dma_start(out=w2t, in_=w2_d).then_inc(qG.h, 16)
            g.dma_start(out=w3t, in_=w3_d).then_inc(qG.h, 16)
            g.dma_start(out=fbrow, in_=fb_d.rearrange("(p f) -> p f", p=1)).then_inc(qG.h, 16)
            g.dma_start(out=boutrow, in_=bout_d.rearrange("(p f) -> p f", p=1)).then_inc(qG.h, 16)
            qG.v += 7 * 16
            rec(("qg_hfilt",), qG.v)
            g.wait_ge(qG.h, qG.v)

            for lo, hi in ((0, 8), (8, 16)):
                eo = 0 if lo == 0 else 2048
                g.dma_start(out=bincols[:, lo:hi],
                            in_=b_in_d[eo:eo + 1024].rearrange("(a p) -> p a", p=128),
                            allow_slow_non_contiguous=True).then_inc(qG.h, 16)
                g.dma_start(out=bccols[:, lo:hi],
                            in_=bconv_d[eo:eo + 1024].rearrange("(a p) -> p a", p=128),
                            allow_slow_non_contiguous=True).then_inc(qG.h, 16)
                g.dma_start(out=wtap[:, lo:hi, :],
                            in_=wconv_d.rearrange("(a p) t -> p a t", p=128)[:, eo // 128:eo // 128 + 8, :]).then_inc(qG.h, 16)
                qG.v += 3 * 16
            g.wait_ge(qG.h, qG.v)

            for j in range(5, NB, 2):
                W(g, ac, pm("xnc", j - 4))
                qXA[j % 4].inc(g.dma_start(out=xld[:, j % 4, :],
                                           in_=x_d[128 * j:128 * j + 128, :]), 16)
                rec(("qsx", j), qXA[j % 4].v)

            W(g, ac, pm("ac_hfev"))
            g.dma_start(out=hpad_d[128:1152].rearrange("(a p) -> p a", p=128),
                        in_=hfsb, allow_slow_non_contiguous=True).then_inc(qG.h, 16)
            g.dma_start(out=hpad_d[0:128].rearrange("(p f) -> p f", p=1), in_=z128).then_inc(qG.h, 16)
            g.dma_start(out=hpad_d[1152:1280].rearrange("(p f) -> p f", p=1), in_=z128).then_inc(qG.h, 16)
            qG.v += 3 * 16
            g.wait_ge(qG.h, qG.v)
            rec(("qg_a0",), qG.v)
            hank = bass.AP(tensor=hpad_d.tensor, offset=1, ap=[[1, 128], [128, 9], [1, 128]])
            g.dma_start(out=hks, in_=hank).then_inc(qH.h, 16)
            qH.v += 16
            rec(("qg_hk",), qH.v)
            g.wait_ge(qH.h, qH.v)

            def g_vin(cc2):
                W(g, dv, pm("taps", 2 * cc2 + 1))
                if cc2 >= 2:
                    W(g, pe, pm("vt", cc2 - 2))
                i = g.tensor_tensor(out=vinbf[:, cc2 % 2, :], in0=s2t,
                                    in1=s0_s[:, cc2, :], op=ALU.mult)
                rec(("vin", cc2), fmark(g, gp, i))

            for k in range(16):
                cs, sh = k // 2, k % 2
                eoff = 128 * cs if sh == 0 else 2048 + 128 * cs
                if k < 2:
                    W(g, dv, pm("trev", 15))
                    W(g, dv, pm("st", 15))
                else:
                    W(g, pe, pm("umm", k - 2))
                if k >= 1:
                    W(g, qG, pm("qgw", k - 1))
                qG.inc(g.dma_start(out=wbuf[:, k % 2],
                                   in_=w_in_d.rearrange("(a p) e -> p a e", p=128)[:, :, eoff:eoff + 128]), 16)
                rec(("qgw", k), qG.v)
                if k >= 4 and k % 2 == 0:
                    g_vin((k - 4) // 2)
            for cc2 in (6, 7):
                g_vin(cc2)

            for s in range(2):
                for hc in range(NHC):
                    if s == 0 and hc == 2:
                        W(g, pe, pm("y", 15))
                        qP.inc(g.dma_start(
                            out=wg_p0,
                            in_=wgT_d.rearrange("(a p) h -> p a h", p=128)[:, :, 0:128]), 16)
                        qP.inc(g.dma_start(
                            out=wu_p0,
                            in_=wuT_d.rearrange("(a p) h -> p a h", p=128)[:, :, 0:128]), 16)
                    if s == 1 and hc == 0:
                        continue
                    slot = hc % 2 if s == 0 else (hc + 1) % 2
                    if s == 1 and hc == 1:
                        W(g, pe, pm("gu", 0, 22))
                    if s == 1 and hc == 2:
                        W(g, pe, pm("gu", 0, 23))
                        W(g, qT, pm("qtst", 0, 7))
                    if (s == 0 and hc >= 2) or (s == 1 and hc >= 3):
                        W(g, pe, pm("gu", s, hc - 2))
                    qGU[slot].inc(g.dma_start(
                        out=wgbs[slot],
                        in_=wgT_d.rearrange("(a p) h -> p a h", p=128)[:, :, 128 * hc:128 * hc + 128]), 16)
                    qGU[slot].inc(g.dma_start(
                        out=wubs[slot],
                        in_=wuT_d.rearrange("(a p) h -> p a h", p=128)[:, :, 128 * hc:128 * hc + 128]), 16)
                    rec(("qggu", s, hc), qGU[slot].v)

                    if s == 0 and 4 <= hc <= 19:
                        if hc == 4:
                            W(g, pe, pm("vtr", 15))
                            W(g, dv, pm("gate", 15))
                        if hc < 16:
                            c2 = 2 * (hc - 4)
                            qD.inc(g.dma_start(
                                out=wdb[:, c2:c2 + 2, :],
                                in_=wdT_d.rearrange("(a p) d -> p a d", p=128)[:, c2:c2 + 2, :]), 16)
                        else:
                            c2 = 2 * (hc - 16)
                            qD.inc(g.dma_start(
                                out=woutb[:, c2:c2 + 2, :],
                                in_=wout_d.rearrange("(a p) d -> p a d", p=128)[:, c2:c2 + 2, :]), 16)
                        rec(("qgwd",), qD.v)
        em("g", g_all)

        def y_all(sy):
            for j in range(0, NB, 2):
                if j >= 4:
                    W(sy, ac, pm("xnc", j - 4))
                qXA[j % 4].inc(sy.dma_start(out=xld[:, j % 4, :],
                                            in_=x_d[128 * j:128 * j + 128, :]), 16)
                rec(("qsx", j), qXA[j % 4].v)

            if DEBUG:
                W(sy, ac, pm("a0ev"))
                qT.inc(sy.dma_start(out=dbg["dbg_hf"], in_=hfsb), 16)
                qT.inc(sy.dma_start(out=dbg["dbg_H"], in_=Hs), 16)
                qT.inc(sy.dma_start(out=dbg["dbg_fbrep"], in_=fbrep), 16)
                W(sy, dv, pm("trev", 15))
                qT.inc(sy.dma_start(out=dbg["dbg_xnT"], in_=xnT), 16)
                qT.inc(sy.dma_start(out=dbg["dbg_stats"], in_=stats), 16)
                W(sy, ac, pm("uev", 0))
                qT.inc(sy.dma_start(out=dbg["dbg_u0"], in_=urow0), 16)
                W(sy, dv, pm("taps", 14))
                qT.inc(sy.dma_start(out=dbg["dbg_s0"], in_=s0_s), 16)
                W(sy, ac, pm("vtev", 7))
                qT.inc(sy.dma_start(out=dbg["dbg_vinT"], in_=vinT), 16)
                W(sy, dv, pm("gate", 15))
                qT.inc(sy.dma_start(out=dbg["dbg_v2"], in_=v2_s), 16)
                W(sy, dv, pm("h", 0, 23))
                qT.inc(sy.dma_start(out=dbg["dbg_h0"], in_=h_buf), 16)

            for s in range(2):
                for jl in range(8):
                    j = 8 * s + jl
                    if jl == 0:
                        W(sy, dv, pm("h", s, 23))
                    if jl >= 1 or s == 1:
                        W(sy, dv, pm("oev", s, jl - 1) if jl >= 1 else pm("oev", 0, 7))
                    qX2.inc(sy.dma_start(out=xld2, in_=x_d[128 * j:128 * j + 128, :]), 16)
                    rec(("qsx2", s, jl), qX2.v)
                    W(sy, dv, pm("oev", s, jl))
                    qT.inc(sy.dma_start(out=out_d[128 * j:128 * j + 128, :], in_=outst), 16)
                    rec(("qtst", s, jl), qT.v)
            sy.wait_ge(qT.h, qT.v)
        em("y", y_all)

        def t_all(t):
            W(t, qG, pm("qg_hfilt"))
            W(t, gp, pm("gp_builds"))
            rec(("pe_w1",), pe.inc(t.matmul(P01[0:64, 1:2], w1s, z3, start=True, stop=True)))
            W(t, ac, pm("ac_h1"))
            rec(("pe_ps1",), pe.inc(t.matmul(P01[0:64, 0:1], w2t, h1c, start=True, stop=True)))
            W(t, ac, pm("ac_h2"))
            for j in range(8):
                i = t.matmul(P23[:, j:j + 1], w3t[:, 128 * j:128 * j + 128], b1c,
                             start=True, stop=True)
            rec(("pe_hf",), pe.inc(i))

            for j in range(NB):
                W(t, ac, pm("xnc", j))
                if j == 0:
                    W(t, ac, pm("ac_h2"))
                elif j == 1:
                    W(t, ac, pm("ac_hfev"))
                else:
                    W(t, ac if j % 2 == 0 else dv, pm("trev", j - 2))
                ps = P01 if j % 2 == 0 else P23
                view = bf16v(ps, 0, 512).rearrange("p (a f) -> p a f", f=128)
                for a in range(8):
                    i = t.transpose(view[:, a, :], xnch[:, j % 2, 128 * a:128 * a + 128], identb)
                rec(("tr", j), pe.inc(i))

            W(t, qH, pm("qg_hk"))
            W(t, dv, pm("trev", 15))
            W(t, ac, pm("trev", 14))
            for d in range(8):
                t.matmul(P01[:, 128 * d:128 * d + 128], Jb, hks[:, d, :], start=True, stop=True)
            t.matmul(P23[:, 0:128], Jb, hks[:, 8, :], start=True, stop=True)
            t.matmul(P45[:, 0:512], ones1, fbrow[:, 0:512], start=True, stop=True)
            i = t.matmul(P45[:, 512:1024], ones1, fbrow[:, 512:1024], start=True, stop=True)
            rec(("pe_a0",), pe.inc(i))

            for it in range(18):
                if it < 16:
                    k = it
                    cs, sh = k // 2, k % 2
                    W(t, qWB[k % 2], pm("qgw", k))
                    if k < 2:
                        W(t, ac, pm("a0ev"))
                    else:
                        W(t, ac, pm("uev", k - 2))
                    if sh == 1 and cs >= 2:
                        W(t, ac, pm("vtev", cs - 2))
                    pair = (P01, P23) if k % 2 == 0 else (P45, P67)
                    tiles = [pair[0][:, 0:512], pair[0][:, 512:1024],
                             pair[1][:, 0:512], pair[1][:, 512:1024]]
                    for a in range(8):
                        for tc in range(4):
                            i = t.matmul(tiles[tc], wbuf[:, k % 2, a, :],
                                         xnT[:, a, 512 * tc:512 * tc + 512],
                                         start=(a == 0), stop=(a == 7))
                    rec(("umm", k), pe.inc(i))
                if (it >= 4 and it % 2 == 0) or it == 17:
                    cc = (it - 4) // 2 if it < 17 else 7
                    W(t, gp, pm("vin", cc))
                    if cc >= 1:
                        W(t, ac, pm("vtev", cc - 1))
                    W(t, ac, pm("uev", min(2 * cc + 3, 15)))
                    view = bf16v(P45, 0, 1024).rearrange("p (a f) -> p a f", f=128)
                    for jt in range(16):
                        i = t.transpose(view[:, jt, :],
                                        vinbf[:, cc % 2, 128 * jt:128 * jt + 128], identb)
                    rec(("vt", cc), pe.inc(i))

            for j in range(NB + 1):
                if j < NB:
                    if j < 2:
                        W(t, ac, pm("uev", 14 + j))
                        W(t, ac, pm("vtev", 7))
                    else:
                        W(t, dv, pm("vs", j - 2))
                    psy = P01 if j % 2 == 0 else P23
                    dmax = min(j, 8)
                    for n2 in range(2):
                        for d in range(dmax + 1):
                            i = t.matmul(psy[:, 512 * n2:512 * n2 + 512], Hs[:, d, :],
                                         vinT[:, j - d, 512 * n2:512 * n2 + 512],
                                         start=(d == 0), stop=(d == dmax))
                    rec(("y", j), pe.inc(i))
                if j >= 1:
                    jj = j - 1
                    W(t, dv, pm("vs", jj))
                    if jj >= 2:
                        W(t, dv, pm("gate", jj - 2))
                    else:
                        W(t, ac, pm("uev", 15))
                        W(t, ac, pm("vtev", 7))
                    pt = P45 if jj % 2 == 0 else P67
                    ptv = pt.rearrange("p (a f) -> p a f", f=128)
                    for q in range(8):
                        i = t.transpose(ptv[:, q, :], vscr[:, jj % 2, 128 * q:128 * q + 128], identf)
                    rec(("vtr", jj), pe.inc(i))

            for s in range(2):
                scol = 1024 * s
                for hc in range(NHC):
                    if s == 1 and hc == 0:
                        W(t, qP, 32)
                        wg_src, wu_src = wg_p0, wu_p0
                    else:
                        slot = hc % 2 if s == 0 else (hc + 1) % 2
                        W(t, qGU[slot], pm("qggu", s, hc))
                        wg_src, wu_src = wgbs[slot], wubs[slot]
                    if hc >= 2:
                        W(t, ac, pm("sil", s, hc - 2))
                        W(t, dv, pm("h", s, hc - 2))
                    elif s == 0:
                        W(t, dv, pm("gate", 15))
                    else:
                        W(t, dv, pm("oev", 0, 7))
                        W(t, ac, pm("sil", 0, 22 + hc))
                        W(t, dv, pm("h", 0, 22 + hc))
                    psg = P01 if hc % 2 == 0 else P45
                    psu = P23 if hc % 2 == 0 else P67
                    for a in range(8):
                        for n2 in range(2):
                            t.matmul(psg[:, 512 * n2:512 * n2 + 512], wg_src[:, a, :],
                                     xnT[:, a, scol + 512 * n2:scol + 512 * n2 + 512],
                                     start=(a == 0), stop=(a == 7))
                    for a in range(8):
                        for n2 in range(2):
                            i = t.matmul(psu[:, 512 * n2:512 * n2 + 512], wu_src[:, a, :],
                                         xnT[:, a, scol + 512 * n2:scol + 512 * n2 + 512],
                                         start=(a == 0), stop=(a == 7))
                    rec(("gu", s, hc), pe.inc(i))

                for jl in range(8):
                    j = 8 * s + jl
                    W(t, qD, pm("qgwd"))
                    if jl >= 2:
                        W(t, dv, pm("oev", s, jl - 2))
                    else:
                        W(t, ac, pm("sil", s, 23))
                        W(t, dv, pm("h", s, 23))
                    pso = P01 if jl % 2 == 0 else P23
                    first = [True, True]
                    for cc in range(8):
                        for n2 in range(2):
                            t.matmul(pso[:, 512 * n2:512 * n2 + 512],
                                     v2_s[:, cc, 128 * j:128 * j + 128],
                                     woutb[:, cc, 512 * n2:512 * n2 + 512],
                                     start=first[n2], stop=False)
                            first[n2] = False
                    for n2 in range(2):
                        t.matmul(pso[:, 512 * n2:512 * n2 + 512], ones1,
                                 boutrow[:, 512 * n2:512 * n2 + 512], start=False, stop=False)
                    for hc in range(NHC):
                        for n2 in range(2):
                            i = t.matmul(pso[:, 512 * n2:512 * n2 + 512],
                                         h_buf[:, hc, 128 * jl:128 * jl + 128],
                                         wdb[:, hc, 512 * n2:512 * n2 + 512],
                                         start=False, stop=(hc == NHC - 1))
                    rec(("guo", s, jl), pe.inc(i))
        em("t", t_all)

        def s_all(s):
            W(s, qG, pm("qg_hfilt"))
            W(s, pe, pm("pe_w1"))
            rec(("ac_h1",), fmark(s, ac, s.activation(out=h1c, in_=P01[0:64, 1:2], func=AF.Silu,
                                                       bias=b1c, scale=1.0)))
            W(s, pe, pm("pe_ps1"))
            rec(("ac_h2",), fmark(s, ac, s.activation(out=b1c, in_=P01[0:64, 0:1], func=AF.Silu,
                                                       bias=b2c, scale=1.0)))
            W(s, pe, pm("pe_hf"))
            rec(("ac_hfev",), fmark(s, ac, s.copy(hfsb, P23[:, 0:8])))

            for it in range(NB + 2):
                if it < NB:
                    j = it
                    W(s, dv, pm("st", j))
                    if j == 0:
                        W(s, gp, pm("gp_builds"))
                    rec(("sd", j), fmark(s, ac, s.activation(out=stats[:, 5, j:j + 1],
                                                              in_=mv[:, j, 1:2],
                                                              func=AF.Sqrt, bias=EPS)))
                if 1 <= it <= NB:
                    j = it - 1
                    W(s, dv, pm("mrs", j))
                    if j >= 2:
                        W(s, pe, pm("tr", j - 2))
                    i = s.activation(out=xnch[:, j % 2, :], in_=xld[:, j % 4, :],
                                     func=AF.Identity, scale=stats[:, 4, j:j + 1],
                                     bias=stats[:, 3, j:j + 1])
                    rec(("xnc", j), ac.inc(i))
                if it >= 2 and (it - 2) % 2 == 0:
                    j = it - 2
                    W(s, pe, pm("tr", j))
                    i = s.copy(xnT[:, :, 128 * j:128 * j + 128],
                               bf16v(P01, 0, 512).rearrange("p (a f) -> p a f", f=128))
                    rec(("trev", j), ac.inc(i))

            W(s, pe, pm("pe_a0"))
            s.copy(Hs[:, 0:8, :], bf16v(P01, 0, 512).rearrange("p (a f) -> p a f", f=128))
            s.copy(Hs[:, 8, :], bf16v(P23, 0, 64))
            rec(("a0ev",), ac.inc(s.copy(fbrep, P45)))

            for k in range(16):
                cs, sh = k // 2, k % 2
                cidx = cs + 8 * sh
                if k == 0:
                    W(s, qG, pm("qg_a0"))
                W(s, pe, pm("umm", k))
                if k >= 2:
                    W(s, dv, pm("taps", k - 2))
                pair = (P01, P23) if k % 2 == 0 else (P45, P67)
                ur = urows[k % 2]
                s.activation(out=ur[:, 2:1026], in_=pair[0],
                             func=AF.Identity, bias=bincols[:, cidx:cidx + 1], scale=1.0)
                i = s.activation(out=ur[:, 1026:2050], in_=pair[1],
                                 func=AF.Identity, bias=bincols[:, cidx:cidx + 1], scale=1.0)
                drain(s, ac, i)
                i = s.activation(out=s_as[k % 2], in_=ur[:, 2:2050], func=AF.Identity,
                                 bias=bccols[:, cidx:cidx + 1], scale=wtap[:, cidx, 2:3])
                rec(("uev", k), fmark(s, ac, i))
                if (k >= 4 and k % 2 == 0) or k == 15:
                    for cc in ([(k - 4) // 2] if k < 15 else [6, 7]):
                        W(s, pe, pm("vt", cc))
                        s.copy(vinT[:, 0:8, 128 * cc:128 * cc + 128],
                               bf16v(P45, 0, 512).rearrange("p (a f) -> p a f", f=128))
                        i = s.copy(vinT[:, 8:16, 128 * cc:128 * cc + 128],
                                   bf16v(P45, 512, 512).rearrange("p (a f) -> p a f", f=128))
                        rec(("vtev", cc), ac.inc(i))

            for s2 in range(2):
                for hc in range(NHC):
                    W(s, pe, pm("gu", s2, hc))
                    if hc >= 1:
                        W(s, dv, pm("h", s2, hc - 1))
                    elif s2 == 1:
                        W(s, dv, pm("oev", 0, 7))
                    psg = P01 if hc % 2 == 0 else P45
                    rec(("sil", s2, hc), ac.inc(
                        s.activation(out=sil_s, in_=psg, func=AF.Silu, scale=SC127)))
        em("s", s_all)

        def v_all(v):
            for it in range(NB + 2):
                if it < NB:
                    j = it
                    W(v, qXA[j % 4], pm("qsx", j))
                    v.bn_stats(bns[:, 0, :], xld[:, j % 4, 0:512])
                    drain(v, dv, v.bn_stats(bns[:, 1, :], xld[:, j % 4, 512:1024]))
                    rec(("st", j), fmark(v, dv, v.bn_aggr(mv[:, j, :], bns.rearrange("p c s -> p (c s)"))))
                if 1 <= it <= NB:
                    j = it - 1
                    W(v, ac, pm("sd", j))
                    drain(v, dv, v.reciprocal(stats[:, 4, j:j + 1], stats[:, 5, j:j + 1]))
                    i = v.tensor_scalar(out=stats[:, 3, j:j + 1], in0=mv[:, j, 0:1],
                                        scalar1=stats[:, 4, j:j + 1], scalar2=-1.0,
                                        op0=ALU.mult, op1=ALU.mult)
                    rec(("mrs", j), fmark(v, dv, i))
                if it >= 2 and (it - 2) % 2 == 1:
                    j = it - 2
                    W(v, pe, pm("tr", j))
                    i = v.tensor_copy(xnT[:, :, 128 * j:128 * j + 128],
                                      bf16v(P23, 0, 512).rearrange("p (a f) -> p a f", f=128))
                    rec(("trev", j), dv.inc(i))

            W(v, pe, pm("tr", 15))
            v.memset(urow0[:, 0:2], 0.0)
            drain(v, dv, v.memset(urow1[:, 0:2], 0.0))
            for it in range(18):
                if 1 <= it <= 16:
                    kk = it - 1
                    csk, shk = kk // 2, kk % 2
                    cidxk = csk + 8 * shk
                    if kk == 0:
                        W(v, qG, pm("qg_a0"))
                    W(v, ac, pm("uev", kk))
                    drain(v, dv, v.scalar_tensor_tensor(out=s_b, in0=urows[kk % 2][:, 1:2049],
                                                        scalar=wtap[:, cidxk, 1:2],
                                                        in1=s_as[kk % 2], op0=ALU.mult, op1=ALU.add))
                    if shk == 1 and csk >= 1:
                        W(v, gp, pm("vin", csk - 1))
                    dest = s0_s[:, csk, :] if shk == 0 else s2t
                    i = v.scalar_tensor_tensor(out=dest, in0=urows[kk % 2][:, 0:2048],
                                               scalar=wtap[:, cidxk, 0:1],
                                               in1=s_b, op0=ALU.mult, op1=ALU.add)
                    if shk == 0:
                        rec(("taps", kk), dv.inc(i))
                        v.wait_ge(dv.h, dv.v)
                    else:
                        rec(("taps", kk), fmark(v, dv, i))

            for j in range(NB + 1):
                if j < NB:
                    W(v, pe, pm("y", j))
                    drain(v, dv, v.tensor_tensor(out=skpt, in0=vinT[:, j, :], in1=fbrep,
                                                 op=ALU.mult))
                    if j >= 2:
                        W(v, pe, pm("vtr", j - 2))
                    psy = P01 if j % 2 == 0 else P23
                    i = v.tensor_tensor(out=vscr[:, j % 2, :], in0=skpt, in1=psy, op=ALU.add)
                    rec(("vs", j), fmark(v, dv, i))
                if j >= 1:
                    jj = j - 1
                    W(v, pe, pm("vtr", jj))
                    pt = P45 if jj % 2 == 0 else P67
                    for h2 in range(2):
                        i = v.tensor_tensor(
                            out=v2_s[:, 4 * h2:4 * h2 + 4, 128 * jj:128 * jj + 128],
                            in0=pt[:, 512 * h2:512 * h2 + 512].rearrange("p (a f) -> p a f", f=128),
                            in1=s0_s[:, 4 * h2:4 * h2 + 4, 128 * jj:128 * jj + 128],
                            op=ALU.mult)
                    rec(("gate", jj), dv.inc(i))

            for s in range(2):
                for hc in range(NHC):
                    W(v, ac, pm("sil", s, hc))
                    psu = P23 if hc % 2 == 0 else P67
                    i = v.scalar_tensor_tensor(out=h_buf[:, hc, :], in0=psu, scalar=SC127SQ,
                                               in1=sil_s, op0=ALU.mult, op1=ALU.mult)
                    rec(("h", s, hc), dv.inc(i))
                for jl in range(8):
                    W(v, pe, pm("guo", s, jl))
                    W(v, qX2, pm("qsx2", s, jl))
                    if jl >= 1 or s == 1:
                        W(v, qT, pm("qtst", s, jl - 1) if jl >= 1 else pm("qtst", 0, 7))
                    pso = P01 if jl % 2 == 0 else P23
                    i = v.tensor_tensor(out=outst, in0=pso, in1=xld2, op=ALU.add)
                    rec(("oev", s, jl), fmark(v, dv, i))
        em("v", v_all)

        blk.gpsimd(lambda g: [f(g) for f in streams["g"]])
        blk.sync(lambda sy: [f(sy) for f in streams["y"]])
        blk.tensor(lambda t: [f(t) for f in streams["t"]])
        blk.scalar(lambda s: [f(s) for f in streams["s"]])
        blk.vector(lambda v: [f(v) for f in streams["v"]])

    return nc, marks


def kernel(**inputs):
    f32 = np.float32
    x = np.ascontiguousarray(np.asarray(inputs["L0_attn_x"], f32))
    w_in = np.ascontiguousarray(np.asarray(inputs["L0_attn_w_in"], f32))
    b_in = np.ascontiguousarray(np.asarray(inputs["L0_attn_b_in"], f32))
    w_conv = np.ascontiguousarray(np.asarray(inputs["L0_attn_w_conv"], f32).reshape(3072, 3))
    b_conv = np.ascontiguousarray(np.asarray(inputs["L0_attn_b_conv"], f32))
    w1 = np.ascontiguousarray(np.asarray(inputs["L0_attn_w1"], f32))
    b1 = np.ascontiguousarray(np.asarray(inputs["L0_attn_b1"], f32))
    w2 = np.ascontiguousarray(np.asarray(inputs["L0_attn_w2"], f32))
    b2 = np.ascontiguousarray(np.asarray(inputs["L0_attn_b2"], f32))
    w3 = np.ascontiguousarray(np.asarray(inputs["L0_attn_w3"], f32))
    fb = np.ascontiguousarray(np.asarray(inputs["L0_attn_filt_bias"], f32))
    w_out = np.ascontiguousarray(np.asarray(inputs["L0_attn_w_out"], f32))
    b_out = np.ascontiguousarray(np.asarray(inputs["L0_attn_b_out"], f32))
    wgT = np.ascontiguousarray(np.asarray(inputs["L0_ffn_w_gate"], np.int32).T)
    wuT = np.ascontiguousarray(np.asarray(inputs["L0_ffn_w_up"], np.int32).T)
    wdT = np.ascontiguousarray(np.asarray(inputs["L0_ffn_w_down"], np.int32).T)

    _, m1 = build_nc(None)
    nc, _ = build_nc(m1)
    in_maps = []
    for b in range(8):
        in_maps.append(dict(x=np.ascontiguousarray(x[b]), w_in=w_in, b_in=b_in,
                            w_conv=w_conv, b_conv=b_conv, w1=w1, b1=b1, w2=w2, b2=b2, w3=w3,
                            filt_bias=fb, w_out=w_out, b_out=b_out,
                            wgT=wgT, wuT=wuT, wdT=wdT))
    res = run_bass_kernel_spmd(nc, in_maps, core_ids=list(range(8)))
    return np.stack([np.asarray(r["out"], np.float32) for r in res.results], axis=0)


if __name__ == "__main__":
    import reference as R
    import jax
    with jax.default_device(jax.devices("cpu")[0]):
        inp = {k: np.asarray(v) for k, v in R.setup_inputs(0).items()}
    out = kernel(**inp)
    print("out", out.shape, out.dtype)
